# version 38
# baseline (speedup 1.0000x reference)
import contextlib
import numpy as np
import ml_dtypes
import concourse.bass as bass
import concourse.mybir as mybir
from concourse.bass_utils import run_bass_kernel_spmd

F32 = mybir.dt.float32
BF16 = mybir.dt.bfloat16
AF = mybir.ActivationFunctionType
ALU = mybir.AluOpType
AX = mybir.AxisListType

B, S, DM, H, DH, FF = 4, 8192, 1024, 8, 128, 4096
KC = DM // 128
TILE = 512
NT = S // TILE
NB = TILE // 128
SW = 136
TB = 256
NTB = (S // 2) // TB
FCH = FF // 128
HL = H // 2
GROUPS = [[0, 1], [2, 3], [4, 5], [6, 7]]
LN_EPS = 1e-5
RMS_EPS = 1e-5
INV_NW = 1.0 / (FF * DM)


class Plan:

    ENGS = ("sync", "gpsimd", "vector", "scalar", "tensor")

    def __init__(self, sems, counts=None):
        self.sems = sems
        self.counts = dict(counts) if counts else {k: 0 for k in sems}
        self.ops = {e: [] for e in self.ENGS}

    def add(self, eng, fn, inc=None, dn=1, waits=()):
        ws = []
        for w in waits:
            if w is None:
                continue
            if isinstance(w, list):
                ws.extend(x for x in w if x is not None)
            else:
                ws.append(w)
        for (s, v) in ws:
            assert v <= self.counts[s], f"wait on future {s}={v}>{self.counts[s]}"
        ev = None
        if inc is not None:
            self.counts[inc] += dn
            ev = (inc, self.counts[inc])
        self.ops[eng].append((fn, inc, dn, tuple(ws)))
        return ev

    def realize(self, eng_name, eng):
        seen = {k: 0 for k in self.sems}
        for fn, inc, dn, ws in self.ops[eng_name]:
            for (s, v) in ws:
                if v > seen[s]:
                    eng.wait_ge(self.sems[s], v)
                    seen[s] = v
            inst = fn(eng)
            if inc is not None:
                inst.then_inc(self.sems[inc], dn)


def build():
    nc = bass.Bass("TRN2", target_bir_lowering=False, detect_race_conditions=False)
    x_d = nc.dram_tensor("x", [S, DM], F32, kind="ExternalInput").ap()
    wqT = nc.dram_tensor("wqT", [DM, HL * DH], F32, kind="ExternalInput").ap()
    wkT = nc.dram_tensor("wkT", [DM, HL * DH], F32, kind="ExternalInput").ap()
    wvT = nc.dram_tensor("wvT", [DM, HL * DH], F32, kind="ExternalInput").ap()
    woT = nc.dram_tensor("woT", [DM, DM], F32, kind="ExternalInput").ap()
    w1T = nc.dram_tensor("w1T", [DM, FF], F32, kind="ExternalInput").ap()
    w2T = nc.dram_tensor("w2T", [FF, DM], F32, kind="ExternalInput").ap()
    b1c = nc.dram_tensor("b1c", [128, FCH], F32, kind="ExternalInput").ap()
    b2r = nc.dram_tensor("b2r", [1, DM], F32, kind="ExternalInput").ap()
    nwb_d = nc.dram_tensor("nwb", [128, 1], F32, kind="ExternalInput").ap()
    xh_d = nc.dram_tensor("xh", [S // 2, DM], F32, kind="ExternalInput").ap()
    hsl_d = nc.dram_tensor("hsl", [1, 2], mybir.dt.int32, kind="ExternalInput").ap()
    out_d = nc.dram_tensor("out", [S // 2, DM], F32, kind="ExternalOutput").ap()

    id_np = np.eye(128, dtype=ml_dtypes.bfloat16)
    maskT_np = np.tile(np.triu(np.ones((128, 128), dtype=np.float32)), (1, 4)).astype(ml_dtypes.bfloat16)
    idb_d = nc.inline_tensor(id_np, "idb_c").ap()
    maskT_d = nc.inline_tensor(maskT_np, "maskT_c").ap()
    ones_cb_d = nc.inline_tensor(np.ones((128, 2), dtype=ml_dtypes.bfloat16), "ones_cb_c").ap()
    ones_rb_d = nc.inline_tensor(np.ones((1, 128), dtype=ml_dtypes.bfloat16), "ones_rb_c").ap()
    ones_cf_d = nc.inline_tensor(np.ones((128, 1), dtype=np.float32), "ones_cf_c").ap()
    ones_rf_d = nc.inline_tensor(np.ones((1, 128), dtype=np.float32), "ones_rf_c").ap()

    onT_snd = [nc.dram_tensor(f"onT_snd{r}", [HL * DH, S // 2], BF16, kind="Internal").ap()
               for r in range(2)]
    onT_g = [nc.dram_tensor(f"onT_g{r}", [DM, S // 2], BF16, kind="Internal").ap()
             for r in range(2)]
    xnT_d = nc.dram_tensor("xnT_s", [DM, S // 2], BF16, kind="Internal").ap()

    sem_names = ["c0", "cg", "x0", "x1", "fx0", "fx1", "fo0", "fo1",
                 "t", "v", "a", "g", "bar", "wB", "wBs", "bx0", "bx1", "bo0", "bo1",
                 "br0", "br1", "bs0", "bs1", "st0", "st1", "t2", "v2", "a2", "g2",
                 "cc1", "cc2", "su0", "su1"]

    with contextlib.ExitStack() as octx:
        oe = octx.enter_context
        sems = {n: oe(nc.semaphore(n)) for n in sem_names}

        idb = oe(nc.sbuf_tensor("idb", [128, 128], BF16))
        maskT = oe(nc.sbuf_tensor("maskT", [128, 512], BF16))
        ones_cb = oe(nc.sbuf_tensor("ones_cb", [128, 2], BF16))
        ones_rb = oe(nc.sbuf_tensor("ones_rb", [1, 128], BF16))
        ones_cf = oe(nc.sbuf_tensor("ones_cf", [128, 1], F32))
        ones_rf = oe(nc.sbuf_tensor("ones_rf", [1, 128], F32))
        nwb = oe(nc.sbuf_tensor("nwb_sb", [128, 1], F32))
        hsl = oe(nc.sbuf_tensor("hsl_sb", [1, 2], mybir.dt.int32))
        w1t = oe(nc.sbuf_tensor("w1t", [128, KC * FF], BF16))
        stage = [oe(nc.sbuf_tensor(f"stg{i}", [128, 1024], F32)) for i in range(2)]
        tmpb = oe(nc.sbuf_tensor("tmpb", [128, 1024], BF16))
        accr = oe(nc.sbuf_tensor("accr", [128, 68], F32))
        thr4 = oe(nc.sbuf_tensor("thr4", [1, 4], F32))
        sc12 = oe(nc.sbuf_tensor("sc12", [1, 2], F32))
        totc = oe(nc.sbuf_tensor("totc", [1, 4], F32))
        eps2 = oe(nc.sbuf_tensor("eps2", [1, 2], F32))
        thrb = oe(nc.sbuf_tensor("thrb", [128, 4], F32))

        with contextlib.ExitStack() as actx:
            e = actx.enter_context
            wq = e(nc.sbuf_tensor("wq", [128, KC * HL * DH], BF16))
            wk = e(nc.sbuf_tensor("wk", [128, KC * HL * DH], BF16))
            wv = e(nc.sbuf_tensor("wv", [128, KC * HL * DH], BF16))
            Sf = e(nc.sbuf_tensor("Sf", [128, HL * SW], F32))
            Sbf = [e(nc.sbuf_tensor(f"Sbf{i}", [128, HL * SW], BF16)) for i in range(2)]
            dbf_s = e(nc.sbuf_tensor("dbf_s", [128, HL * 128], BF16))
            dbf_z = e(nc.sbuf_tensor("dbf_z", [128, HL * 2], BF16))
            xt = [e(nc.sbuf_tensor(f"xt{i}", [128, NB * DM], F32)) for i in range(2)]
            xnf = [e(nc.sbuf_tensor(f"xnf{i}", [128, KC * TILE], BF16)) for i in range(2)]
            qf = e(nc.sbuf_tensor("qf", [128, HL * TILE], BF16))
            kf = e(nc.sbuf_tensor("kf", [128, HL * TILE], BF16))
            vt = e(nc.sbuf_tensor("vt", [128, NB * HL * DH], BF16))
            ktb = e(nc.sbuf_tensor("ktb", [128, NB * HL * 128], BF16))
            onsb = e(nc.sbuf_tensor("onsb", [128, NB * HL * 128], BF16))
            osb = e(nc.sbuf_tensor("osb", [128, NB * 512], F32))
            onT = [e(nc.sbuf_tensor(f"onT{i}", [128, HL * TILE], BF16)) for i in range(2)]
            xn_tm = [e(nc.sbuf_tensor(f"xn_tm{i}", [128, DM], BF16)) for i in range(2)]
            sq_scr = e(nc.sbuf_tensor("sq_scr", [128, DM], BF16))
            id_scr = e(nc.sbuf_tensor("id_scr", [128, DM], BF16))
            sc_sb = e(nc.sbuf_tensor("sc_sb", [128, 512], BF16))
            stats = e(nc.sbuf_tensor("stats", [128, 64], F32))
            atts = e(nc.sbuf_tensor("atts", [128, 320], F32))
            ones16 = e(nc.sbuf_tensor("ones16", [128, 32], F32))
            epsc = e(nc.sbuf_tensor("epsc", [128, 1], F32))
            psA = [e(nc.psum_tensor(f"psA{i}", [128, 512], F32)) for i in range(2)]
            psO = [e(nc.psum_tensor(f"psO{i}", [128, 512], F32)) for i in range(2)]
            psDen = e(nc.psum_tensor("psDen", [128, 16], F32))
            psSd = e(nc.psum_tensor("psSd", [128, 512], F32))
            psT2 = [e(nc.psum_tensor(f"psT{i}", [128, 512], BF16)) for i in range(2)]

            pl = Plan(sems)

            for dst, src in ((idb, idb_d), (maskT, maskT_d), (ones_cb, ones_cb_d),
                             (ones_rb, ones_rb_d), (ones_cf, ones_cf_d),
                             (ones_rf, ones_rf_d), (nwb, nwb_d), (hsl, hsl_d)):
                pl.add("sync", lambda s, d=dst, sr=src: s.dma_start(out=d[:, :], in_=sr),
                       inc="c0", dn=16)
            for dst, src in ((wq, wqT), (wk, wkT), (wv, wvT)):
                ap = src.rearrange("(kc p) e -> p kc e", p=128)
                dap = dst[:, :].rearrange("p (kc e) -> p kc e", kc=KC)

                pl.add("gpsimd", lambda g, d=dap, sr=ap: g.dma_start(out=d, in_=sr),
                       inc="cg", dn=16)
            EV_C0 = [("c0", pl.counts["c0"]), ("cg", pl.counts["cg"])]

            pl.add("vector", lambda v: v.memset(ones16[:, :], 1.0), inc="v")
            pl.add("vector", lambda v: v.memset(epsc[:, :], LN_EPS), inc="v")
            regs = {}

            def _mk_load(engname, enum):
                def f(eng):
                    regs[(engname, 0)] = nc.values_load(
                        hsl[0:1, 0:1], engines=[enum], skip_runtime_bounds_check=True)
                    regs[(engname, 1)] = nc.values_load(
                        hsl[0:1, 1:2], engines=[enum], skip_runtime_bounds_check=True)
                    return eng.nop()
                return f
            pl.add("scalar", _mk_load("scalar", mybir.EngineType.Activation),
                   waits=[("c0", pl.counts["c0"])])
            pl.add("sync", _mk_load("sync", mybir.EngineType.SP),
                   waits=[("c0", pl.counts["c0"])])

            def spill_fn(engname, dst, srcap, want, sem):
                def f(eng):
                    with eng.If(regs[(engname, want)]):
                        eng.dma_start(out=dst, in_=srcap).then_inc(sems[sem], 16)
                    with eng.Else():
                        eng.sem_inc(sems[sem], 16)
                    return eng.nop()
                return f
            pl.add("vector", lambda v: v.memset(atts[:, :], 0.0), inc="v")
            pl.add("vector", lambda v: v.memset(eps2[:, :], 1e-8), inc="v")
            pl.add("vector", lambda v: v.memset(Sf[:, :], 0.0), inc="v")
            pl.add("vector", lambda v: v.memset(Sbf[0][:, :], 0.0), inc="v")
            pl.add("vector", lambda v: v.memset(Sbf[1][:, :], 0.0), inc="v")
            pl.add("vector", lambda v: v.memset(dbf_s[:, :], 0.0), inc="v")
            ev_sb0 = pl.add("vector", lambda v: v.memset(dbf_z[:, :], 0.0), inc="v")

            tern_state = {"stf": [None, None], "ld": 0}

            def tern_stage_load(src_ap, sl):
                w = [tern_state["stf"][sl], EV_C0]
                return pl.add("scalar", lambda s, sr=src_ap, sl_=sl: s.dma_start(
                    out=stage[sl_][:, :], in_=sr), inc=f"st{sl}", dn=16, waits=w)

            def emit_scale_chunk(c):
                if c < 32:
                    srcp = w1T[(c % 8) * 128:(c % 8) * 128 + 128,
                               (c // 8) * 1024:(c // 8) * 1024 + 1024]
                else:
                    srcp = w2T[(c - 32) * 128:(c - 31) * 128, :]
                sl = c % 2
                ev_ld = tern_stage_load(srcp, sl)
                ev_r = pl.add("vector", lambda v, c_=c, sl_=sl: v.tensor_reduce(
                    accr[:, c_:c_ + 1], stage[sl_][:, :], AX.X, ALU.add,
                    apply_absolute_value=True), inc="v", waits=[ev_ld])
                tern_state["stf"][sl] = ev_r

            w1rdy = [None] * 4

            def emit_w1_chunk(ct, ev_thrb_):
                fi, kc = ct // 8, ct % 8
                srcp = w1T[kc * 128:(kc + 1) * 128, fi * 1024:(fi + 1) * 1024]
                sl = ct % 2
                ev_ld = tern_stage_load(srcp, sl)
                pl.add("vector", lambda v, sl_=sl: v.tensor_single_scalar(
                    tmpb[:, :], stage[sl_][:, :], thrb[:, 1:2], ALU.is_lt),
                    inc="v", waits=[ev_ld, ev_thrb_])
                ev_w = pl.add("vector", lambda v, kc_=kc, fi_=fi, sl_=sl: v.scalar_tensor_tensor(
                    w1t[:, kc_ * FF + fi_ * 1024: kc_ * FF + fi_ * 1024 + 1024],
                    stage[sl_][:, :], thrb[:, 0:1], tmpb[:, :],
                    ALU.is_gt, ALU.subtract), inc="v")
                tern_state["stf"][sl] = ev_w
                if kc == 7:
                    w1rdy[fi] = ev_w
                return ev_w

            xt_free = [None, None]
            xnf_free = [None, None]
            onT_free = [None, None]
            last_sbf = ev_sb0
            sbf_ev = [ev_sb0, ev_sb0]
            dbf_ev = ev_sb0
            psA_ev = [None, None]
            psO_ev = [None, None]
            psDen_ev = None
            psSd_ev = None
            psT_free = [None, None]
            last_stats_w = None
            last_var_g = None
            last_ms_ev = None
            last_dn_ev = None
            last_ep_act = None
            last_ab = None
            osb_war = None
            last_xntp = None
            blkctr = 0

            def emit_xload(ti):
                par2 = ti % 2
                evs = []
                for b in range(NB):
                    srcx = x_d[ti * TILE + b * 128: ti * TILE + (b + 1) * 128, :]
                    dapx = xt[par2][:, b * DM:(b + 1) * DM]
                    ev = pl.add("sync", lambda s, d=dapx, sr=srcx: s.dma_start(
                        out=d, in_=sr), inc=f"x{par2}", dn=16,
                        waits=[xt_free[par2]] if b == 0 else [])
                    evs.append(ev)
                xload_ev[par2] = evs

            def emit_ln_stats(ti, j):
                nonlocal last_stats_w, last_var_g
                par2 = ti % 2
                B0 = 32 * par2
                xblk = xt[par2][:, j * DM:(j + 1) * DM]
                pl.add("scalar", lambda s, xb=xblk, j_=j, b_=B0: s.activation(
                    sq_scr[:, :], xb, AF.Square, scale=1.0 / 32.0,
                    accum_out=stats[:, b_ + j_:b_ + j_ + 1]), inc="a",
                    waits=[xload_ev[par2][j]] + ([last_stats_w, last_var_g] if j == 0 else []))
                lnp[(ti, j)] = pl.add("scalar", lambda s, xb=xblk, j_=j, b_=B0: s.activation(
                    id_scr[:, :], xb, AF.Identity, scale=1.0 / DM,
                    accum_out=stats[:, b_ + 4 + j_:b_ + 5 + j_]), inc="a")

            def emit_ln_chain_j(ti, j):
                nonlocal last_var_g
                par2 = ti % 2
                B0 = 32 * par2
                ev_sqid = lnp.pop((ti, j))
                ev_m2 = pl.add("vector", lambda v, b_=B0, j_=j: v.tensor_tensor(
                    stats[:, b_ + 8 + j_:b_ + 9 + j_], stats[:, b_ + 4 + j_:b_ + 5 + j_],
                    stats[:, b_ + 4 + j_:b_ + 5 + j_], ALU.mult),
                    inc="v", waits=[ev_sqid] + ([last_var_g] if j == 0 else []))
                ev_var = pl.add("gpsimd", lambda g, b_=B0, j_=j: g.tensor_tensor(
                    stats[:, b_ + 12 + j_:b_ + 13 + j_], stats[:, b_ + j_:b_ + 1 + j_],
                    stats[:, b_ + 8 + j_:b_ + 9 + j_], ALU.subtract),
                    inc="g", waits=[ev_m2])
                if j == NB - 1:
                    last_var_g = ev_var
                ev_sdl = pl.add("scalar", lambda s, b_=B0, j_=j: s.activation(
                    stats[:, b_ + 16 + j_:b_ + 17 + j_],
                    stats[:, b_ + 12 + j_:b_ + 13 + j_],
                    AF.Sqrt, bias=epsc[:, 0:1]), inc="a", waits=[ev_var])
                ev_rr = pl.add("vector", lambda v, b_=B0, j_=j: v.reciprocal(
                    stats[:, b_ + 20 + j_:b_ + 21 + j_],
                    stats[:, b_ + 16 + j_:b_ + 17 + j_]),
                    inc="v", waits=[ev_sdl])
                ev_rs = pl.add("scalar", lambda s, b_=B0, j_=j: s.copy(
                    stats[:, b_ + 24 + j_:b_ + 25 + j_],
                    stats[:, b_ + 20 + j_:b_ + 21 + j_]),
                    inc="a", waits=[ev_rr])
                lnp[(ti, "rs", j)] = ev_rs

            def emit_ln_xn_j(ti, j):
                nonlocal last_stats_w, xnf_ready, xnf_free
                par2 = ti % 2
                B0 = 32 * par2
                ev_rs = lnp.pop((ti, "rs", j))
                bp2 = j % 2
                xblk = xt[par2][:, j * DM:(j + 1) * DM]
                ev_xn = pl.add("vector", lambda v, xb=xblk, j_=j, b_=B0, p_=bp2: v.tensor_scalar(
                    xn_tm[p_][:, :], xb, stats[:, b_ + 4 + j_:b_ + 5 + j_],
                    stats[:, b_ + 24 + j_:b_ + 25 + j_],
                    ALU.subtract, ALU.mult), inc="v",
                    waits=[ev_rs, last_xn_tp[bp2]])
                ev_last_evac = None
                for q in range(2):
                    sl = q
                    ev_tp = None
                    for c4 in range(4):
                        c = q * 4 + c4
                        ev_tp = pl.add("tensor", lambda t, c_=c, c4_=c4, sl_=sl, p_=bp2: t.transpose(
                            psT2[sl_][:, c4_ * 128:(c4_ + 1) * 128],
                            xn_tm[p_][:, c_ * 128:(c_ + 1) * 128], idb[:, :]),
                            inc=("t" if c4 == 3 else None),
                            waits=([ev_xn, psT_free[sl]] if c4 == 0 else []))
                    ev_ev = pl.add("scalar", lambda s, q_=q, j_=j, sl_=sl, p_=par2: s.copy(
                        xnf[p_][:, :].rearrange("p (c t) -> p c t", c=KC)[:, q_ * 4:q_ * 4 + 4, j_ * 128:j_ * 128 + 128],
                        psT2[sl_][:, :].rearrange("p (c t) -> p c t", c=4)),
                        inc="a", waits=[ev_tp, xnf_free[par2] if j == 0 and q == 0 else None])
                    psT_free[sl] = ev_ev
                    ev_last_evac = ev_ev
                    last_xn_tp[bp2] = ev_tp
                if j < NB - 1:
                    return
                lc = (ti * TILE) % (S // 2)
                want = 0 if ti * TILE < S // 2 else 1
                xnf_free[par2] = None
                last_stats_w = ev_xn
                xt_free[par2] = [ev_xn]
                xnf_ready[par2] = ev_last_evac
                dstap = xnT_d[:, lc:lc + TILE].rearrange("(c p) t -> p c t", p=128)
                srcap = xnf[par2][:, :].rearrange("p (c t) -> p c t", c=KC)
                pl.counts[f"fx{par2}"] += 16
                ev_fx = (f"fx{par2}", pl.counts[f"fx{par2}"])
                pl.add("sync", spill_fn("sync", dstap, srcap, want, f"fx{par2}"),
                       waits=[ev_last_evac])
                xnf_free[par2] = ev_fx

            xload_ev = [None, None]
            xnf_ready = [None, None]
            last_xn_tp = [None, None]
            ring_unused = None
            lnp = {}
            emit_xload(0)
            for b in range(NB):
                emit_ln_stats(0, b)
            for b in range(NB):
                emit_ln_chain_j(0, b)
            for b in range(NB):
                emit_ln_xn_j(0, b)

            for i in range(NT):
                t0 = i * TILE
                par = i % 2
                lc = t0 % (S // 2)
                want = 0 if t0 < S // 2 else 1
                if i + 1 < NT:
                    emit_xload(i + 1)
                ev_xnf_ready = xnf_ready[par]

                k_evac = [None] * HL
                v_evac = [None] * NB
                WH = HL * DH
                for wi, (wmat, dstf) in enumerate(((wq, qf), (wk, kf))):
                    for h in range(HL):
                        sl = h % 2
                        ev_mm = None
                        for kc in range(KC):
                            ev_mm = pl.add("tensor", lambda t, h_=h, kc_=kc, sl_=sl, wm=wmat, p_=par: t.matmul(
                                psA[sl_][:, 0:512],
                                wm[:, kc_ * WH + h_ * 128: kc_ * WH + h_ * 128 + 128],
                                xnf[p_][:, kc_ * TILE:(kc_ + 1) * TILE],
                                start=(kc_ == 0), stop=(kc_ == KC - 1)),
                                inc=("t" if kc == KC - 1 else None),
                                waits=([ev_xnf_ready, EV_C0, psA_ev[sl]] if kc == 0 else []))
                        ev_ev = pl.add("vector", lambda v, h_=h, sl_=sl, df=dstf: v.tensor_copy(
                            df[:, h_ * TILE:(h_ + 1) * TILE], psA[sl_][:, 0:512]),
                            inc="v", waits=[ev_mm])
                        psA_ev[sl] = ev_ev
                        if wi == 1:
                            k_evac[h] = ev_ev
                for j in range(NB):
                    sl = j % 2
                    ev_mm = None
                    for kc in range(KC):
                        ev_mm = pl.add("tensor", lambda t, j_=j, kc_=kc, sl_=sl, p_=par: t.matmul(
                            psA[sl_][:, 0:512],
                            xnf[p_][:, kc_ * TILE + j_ * 128: kc_ * TILE + j_ * 128 + 128],
                            wv[:, kc_ * WH: kc_ * WH + 512],
                            start=(kc_ == 0), stop=(kc_ == KC - 1)),
                            inc=("t" if kc == KC - 1 else None),
                            waits=([ev_xnf_ready, EV_C0, psA_ev[sl]] if kc == 0 else []))
                    ev_ev = pl.add("vector", lambda v, j_=j, sl_=sl: v.tensor_copy(
                        vt[:, j_ * 512: j_ * 512 + 512],
                        psA[sl_][:, 0:512]), inc="v", waits=[ev_mm])
                    psA_ev[sl] = ev_ev
                    v_evac[j] = ev_ev

                ktb_ev = [None] * NB
                for j in range(NB):
                    sl = j % 2
                    ev_ktp = None
                    for h in range(HL):
                        ksl = kf[:, h * TILE + j * 128: h * TILE + j * 128 + 128]
                        ev_ktp = pl.add("tensor", lambda t, ks=ksl, h_=h, sl_=sl: t.transpose(
                            psT2[sl_][:, h_ * 128:(h_ + 1) * 128], ks, idb[:, :]),
                            inc=("t" if h == HL - 1 else None),
                            waits=([k_evac[HL - 1], psT_free[sl]] if h == 0 else []))
                    ev_kte = pl.add("scalar", lambda s, j_=j, sl_=sl: s.copy(
                        ktb[:, j_ * 512:(j_ + 1) * 512],
                        psT2[sl_][:, :]), inc="a", waits=[ev_ktp])
                    psT_free[sl] = ev_kte
                    ktb_ev[j] = ev_kte

                for j in range(NB):
                    if i + 1 < NT and j >= 2:
                        emit_ln_xn_j(i + 1, j - 2)
                    p = (blkctr >> 1) & 1
                    scb = 0
                    ev_sc = None
                    for h in range(HL):
                        qsl = qf[:, h * TILE + j * 128: h * TILE + j * 128 + 128]
                        ksl = kf[:, h * TILE + j * 128: h * TILE + j * 128 + 128]
                        ev_sc = pl.add("tensor", lambda t, ks=ksl, qs=qsl, h_=h, b_=scb: t.matmul(
                            psA[b_][:, h_ * 128:h_ * 128 + 128], ks, qs,
                            start=(h_ == 0), stop=(h_ == HL - 1)),
                            inc=("t" if h == HL - 1 else None),
                            waits=([psA_ev[scb]] if h == 0 else []))
                    psA_ev[scb] = None
                    ev_me = pl.add("vector", lambda v, b_=scb: v.tensor_tensor(
                        sc_sb[:, :], psA[b_][:, 0:512], maskT[:, :], ALU.mult),
                        inc="v", waits=[ev_sc])
                    psA_ev[scb] = ev_me
                    for h in range(HL):
                        qsl = qf[:, h * TILE + j * 128: h * TILE + j * 128 + 128]
                        wts = []
                        if h == 0:
                            wts = [sbf_ev[p], dbf_ev, psO_ev[p]]
                            psO_ev[p] = None
                        pl.add("tensor", lambda t, qs=qsl, h_=h, p_=p: t.matmul(
                            psO[p_][:, h_ * 128:h_ * 128 + 128], qs,
                            Sbf[p_][:, h_ * SW:h_ * SW + 128], start=(h_ == 0), stop=False),
                            waits=wts)
                        pl.add("tensor", lambda t, qs=qsl, h_=h, p_=p: t.matmul(
                            psO[p_][:, h_ * 128:h_ * 128 + 128], qs,
                            dbf_s[:, h_ * 128:h_ * 128 + 128], start=False, stop=False))
                    ev_o2 = None
                    for h in range(HL):
                        vsl = vt[:, j * 512 + h * 128: j * 512 + h * 128 + 128]
                        ev_o2 = pl.add("tensor", lambda t, h_=h, vs=vsl, p_=p: t.matmul(
                            psO[p_][:, h_ * 128:h_ * 128 + 128],
                            sc_sb[:, h_ * 128:h_ * 128 + 128], vs,
                            start=False, stop=(h_ == HL - 1)),
                            inc=("t" if h == HL - 1 else None),
                            waits=([ev_me, v_evac[j]] if h == 0 else []))
                    for h in range(HL):
                        vsl = vt[:, j * 512 + h * 128: j * 512 + h * 128 + 128]
                        pl.add("tensor", lambda t, j_=j, h_=h, vs=vsl: t.matmul(
                            psSd[:, h_ * 128:h_ * 128 + 128],
                            ktb[:, (j_ * HL + h_) * 128:(j_ * HL + h_) * 128 + 128], vs,
                            start=(h_ == 0), stop=(h_ == HL - 1)),
                            waits=([ktb_ev[j], psSd_ev] if h == 0 else []))
                    for h in range(HL):
                        qsl = qf[:, h * TILE + j * 128: h * TILE + j * 128 + 128]
                        pl.add("tensor", lambda t, qs=qsl, h_=h, p_=p: t.matmul(
                            psDen[:, 2 * h_:2 * h_ + 2], qs,
                            Sbf[p_][:, h_ * SW + 128:h_ * SW + 130],
                            start=(h_ == 0), stop=False),
                            waits=([psDen_ev] if h == 0 else []))
                        pl.add("tensor", lambda t, qs=qsl, h_=h: t.matmul(
                            psDen[:, 2 * h_:2 * h_ + 2], qs,
                            dbf_z[:, 2 * h_:2 * h_ + 2],
                            start=False, stop=False))
                    psDen_ev = None
                    for h in range(HL):
                        pl.add("tensor", lambda t, h_=h: t.matmul(
                            psDen[:, 2 * h_:2 * h_ + 1],
                            sc_sb[:, h_ * 128:h_ * 128 + 128], ones_cb[:, 0:1],
                            start=False, stop=False))
                    ev_z = None
                    for h in range(HL):
                        ev_z = pl.add("tensor", lambda t, j_=j, h_=h: t.matmul(
                            psDen[:, 8 + 2 * h_:10 + 2 * h_],
                            ktb[:, (j_ * HL + h_) * 128:(j_ * HL + h_) * 128 + 128],
                            ones_cb[:, 0:2], start=False, stop=(h_ == HL - 1)),
                            inc=("t" if h == HL - 1 else None))
                    ev_dbf1 = pl.add("vector", lambda v: v.tensor_copy(
                        dbf_s[:, :], psSd[:, 0:512]), inc="v", waits=[ev_z])
                    ev_dbf2 = pl.add("vector", lambda v: v.tensor_copy(
                        dbf_z[:, :], psDen[:, 8:16]), inc="v")
                    dbf_ev = ev_dbf2
                    sfv = Sf[:, 0:HL * SW].rearrange("p (h w) -> p h w", w=SW)
                    ev_sa1 = pl.add("vector", lambda v, s_=sfv: v.tensor_tensor(
                        s_[:, :, 0:128], s_[:, :, 0:128],
                        psSd[:, 0:512].rearrange("p (h d) -> p h d", d=128), ALU.add),
                        inc="v", waits=[ev_z, last_sbf])
                    ev_sa2 = pl.add("vector", lambda v, s_=sfv: v.tensor_tensor(
                        s_[:, :, 128:130], s_[:, :, 128:130],
                        psDen[:, 8:16].rearrange("p (h d) -> p h d", d=2), ALU.add),
                        inc="v")
                    psSd_ev = [ev_sa1, ev_dbf1]
                    ev_sbf = pl.add("gpsimd", lambda g, p_=p: g.tensor_copy(
                        Sbf[p_][:, 0:HL * SW], Sf[:, 0:HL * SW]), inc="g",
                        waits=[ev_sa2])
                    last_sbf = ev_sbf
                    sbf_ev[p] = ev_sbf
                    ev_oev = pl.add("scalar", lambda s, j_=j, p_=p: s.copy(
                        osb[:, j_ * 512:(j_ + 1) * 512], psO[p_][:, 0:512]),
                        inc="a", waits=[ev_o2, osb_war if j == 0 else None])
                    psO_ev[p] = ev_oev
                    for h in range(HL):
                        pl.add("scalar", lambda s, j_=j, h_=h: s.activation(
                            sq_scr[:, 0:128], osb[:, j_ * 512 + h_ * 128: j_ * 512 + h_ * 128 + 128],
                            AF.Square, scale=0.08838834764831845,
                            accum_out=atts[:, j_ * 8 + 2 * h_:j_ * 8 + 2 * h_ + 1]),
                            inc=None, waits=[last_ms_ev if j == 0 and h == 0 else None])
                    ev_ab = pl.add("scalar", lambda s, j_=j: s.activation(
                        atts[:, 64 + j_ * 8:72 + j_ * 8], psDen[:, 0:8], AF.Abs), inc="a",
                        waits=[last_dn_ev if j == 0 else None])
                    psDen_ev = [ev_ab, ev_sa2, ev_dbf2]
                    last_ab = ev_ab
                    blkctr += 2
                    if i + 1 < NT:
                        emit_ln_stats(i + 1, j)
                        if j >= 1:
                            emit_ln_chain_j(i + 1, j - 1)

                if i + 1 < NT:
                    emit_ln_chain_j(i + 1, NB - 1)
                    emit_ln_xn_j(i + 1, 2)

                ev_dn = pl.add("vector", lambda v: v.tensor_tensor(
                    atts[:, 128:160], atts[:, 64:96], ones16[:, :], ALU.max),
                    inc="v", waits=[last_ab, last_ep_act])
                last_dn_ev = ev_dn
                ev_ed = pl.add("scalar", lambda s: s.activation(
                    atts[:, 160:192], atts[:, 128:160], AF.Square,
                    scale=0.0031622776601683794), inc="a", waits=[ev_dn])
                ev_ms = pl.add("vector", lambda v: v.tensor_tensor(
                    atts[:, 192:224], atts[:, 160:192], atts[:, 0:32], ALU.add),
                    inc="v", waits=[ev_ed])
                last_ms_ev = ev_ms
                ev_sdm = pl.add("scalar", lambda s: s.activation(
                    atts[:, 224:256], atts[:, 192:224], AF.Sqrt), inc="a",
                    waits=[ev_ms])
                ev_ar = pl.add("vector", lambda v: v.reciprocal(
                    atts[:, 256:288], atts[:, 224:256]), inc="v", waits=[ev_sdm])
                ev_al = pl.add("scalar", lambda s: s.copy(
                    atts[:, 288:304].unsqueeze(-1),
                    atts[:, 256:288].rearrange("p (n two) -> p n two", two=2)[:, :, 0:1]),
                    inc="a", waits=[ev_ar])
                last_ep_act = ev_al
                ev_on_last = None
                on_evs = []
                for j in range(NB):
                    ev_on = None
                    for h in range(HL):
                        col = 288 + j * 4 + h
                        ev_on = pl.add("vector", lambda v, j_=j, h_=h, c_=col: v.tensor_scalar(
                            onsb[:, j_ * 512 + h_ * 128: j_ * 512 + h_ * 128 + 128],
                            osb[:, j_ * 512 + h_ * 128: j_ * 512 + h_ * 128 + 128],
                            atts[:, c_:c_ + 1], None, ALU.mult, ALU.bypass),
                            inc="v", waits=([ev_al] if j == 0 and h == 0 else []))
                    on_evs.append(ev_on)
                    ev_on_last = ev_on
                if i + 1 < NT:
                    emit_ln_xn_j(i + 1, 3)
                for j in range(NB):
                    sl = j % 2
                    ev_otp = None
                    for h in range(HL):
                        ev_otp = pl.add("tensor", lambda t, j_=j, h_=h, sl_=sl: t.transpose(
                            psT2[sl_][:, h_ * 128:(h_ + 1) * 128],
                            onsb[:, j_ * 512 + h_ * 128: j_ * 512 + h_ * 128 + 128],
                            idb[:, :]), inc=("t" if h == HL - 1 else None),
                            waits=([on_evs[j], psT_free[sl]] if h == 0 else []))
                    ev_oevT = pl.add("scalar", lambda s, sl_=sl, j_=j, p_=par: s.copy(
                        onT[p_][:, :].rearrange("p (h t) -> p h t", h=HL)[:, :, j_ * 128:j_ * 128 + 128],
                        psT2[sl_][:, :].rearrange("p (h t) -> p h t", h=HL)),
                        inc="a", waits=[ev_otp, onT_free[par] if j == 0 else None])
                    psT_free[sl] = ev_oevT
                onT_free[par] = None
                osb_war = ev_on_last
                ev_onT_tile = ev_oevT

                dstap = onT_snd[want][:, lc:lc + TILE].rearrange("(h p) t -> p h t", p=128)
                srcap = onT[par][:, :].rearrange("p (h t) -> p h t", h=HL)
                ev_fo = pl.add("scalar", lambda s, d=dstap, sr=srcap: s.dma_start(
                    out=d, in_=sr), inc=f"fo{par}", dn=16, waits=[ev_onT_tile])

                onT_free[par] = ev_fo

                if i == NT // 2 - 1:
                    ev_cc1 = pl.add("gpsimd", lambda g: g.collective_compute(
                        "AllGather", ALU.bypass, GROUPS,
                        [onT_snd[0].opt()], [onT_g[0].opt()]),
                        inc="cc1", dn=1,
                        waits=[("fo0", pl.counts["fo0"]), ("fo1", pl.counts["fo1"])])
                if i == NT - 1:
                    ev_cc2 = pl.add("gpsimd", lambda g: g.collective_compute(
                        "AllGather", ALU.bypass, GROUPS,
                        [onT_snd[1].opt()], [onT_g[1].opt()]),
                        inc="cc2", dn=1,
                        waits=[("fo0", pl.counts["fo0"]), ("fo1", pl.counts["fo1"])])

                if i < 4:
                    for c in range(i * 16, i * 16 + 16):
                        emit_scale_chunk(c)
                elif i == 4:
                    ev_ra1 = pl.add("scalar", lambda s: s.activation(
                        tmpb[:, 0:32], accr[:, 0:32], AF.Identity,
                        accum_out=accr[:, 64:65]), inc="a",
                        waits=[tern_state["stf"][0], tern_state["stf"][1]])
                    ev_ra2 = pl.add("scalar", lambda s: s.activation(
                        tmpb[:, 0:32], accr[:, 32:64], AF.Identity,
                        accum_out=accr[:, 65:66]), inc="a")
                    pl.add("tensor", lambda t: t.matmul(
                        psA[0][0:1, 0:1], accr[:, 64:65], ones_cf[:, :],
                        start=True, stop=False),
                        waits=[ev_ra1, ev_ra2, psA_ev[0]])
                    ev_tot = pl.add("tensor", lambda t: t.matmul(
                        psA[0][0:1, 2:3], accr[:, 65:66], ones_cf[:, :],
                        start=False, stop=True), inc="t")
                    pl.add("vector", lambda v: v.tensor_copy(
                        totc[:, 0:1], psA[0][0:1, 0:1]), inc="v", waits=[ev_tot])
                    ev_tc = pl.add("vector", lambda v: v.tensor_copy(
                        totc[:, 1:2], psA[0][0:1, 2:3]), inc="v")
                    psA_ev[0] = ev_tc
                    ev_sc_a = pl.add("scalar", lambda s: s.mul(
                        totc[:, 2:4], totc[:, 0:2], INV_NW), inc="a", waits=[ev_tc])
                    ev_sc12 = pl.add("vector", lambda v: v.tensor_tensor(
                        sc12[:, 0:2], totc[:, 2:4], eps2[:, :], ALU.max), inc="v",
                        waits=[ev_sc_a])
                    ev_t4 = None
                    for k, (col, mulv) in enumerate(((0, 0.5), (1, -0.5), (2, 0.5), (3, -0.5))):
                        ev_t4 = pl.add("scalar", lambda s, c_=col, m=mulv: s.mul(
                            thr4[:, c_:c_ + 1], sc12[:, c_ // 2:c_ // 2 + 1], m),
                            inc=("a" if k == 3 else None),
                            waits=([ev_sc12] if k == 0 else []))
                    ev_thr = pl.add("tensor", lambda t: t.matmul(
                        psA[1][:, 0:4], ones_rf[:, :], thr4[:, :],
                        start=True, stop=True), inc="t", waits=[ev_t4, psA_ev[1]])
                    ev_thrb = pl.add("scalar", lambda s: s.copy(
                        thrb[:, :], psA[1][:, 0:4]), inc="a", waits=[ev_thr])
                    psA_ev[1] = ev_thrb
                elif i < 13:
                    for ct in range((i - 5) * 4, (i - 5) * 4 + 4):
                        emit_w1_chunk(ct, ev_thrb)

            finalsA = dict(pl.counts)

            with nc.Block() as blkA:
                @blkA.sync
                def _(engx):
                    pl.realize("sync", engx)

                @blkA.gpsimd
                def _(engx):
                    pl.realize("gpsimd", engx)
                    for s in ("fx0", "fx1", "fo0", "fo1", "t", "v", "a"):
                        engx.wait_ge(sems[s], finalsA[s])
                    engx.sem_inc(sems["bar"], 1)

                @blkA.vector
                def _(engx):
                    pl.realize("vector", engx)

                @blkA.scalar
                def _(engx):
                    pl.realize("scalar", engx)

                @blkA.tensor
                def _(engx):
                    pl.realize("tensor", engx)

        with contextlib.ExitStack() as bctx:
            e = bctx.enter_context
            wo = e(nc.sbuf_tensor("wo", [128, H * DM], BF16))
            w2t = e(nc.sbuf_tensor("w2t", [128, FCH * DM], BF16))
            b1s = e(nc.sbuf_tensor("b1s", [128, FCH], F32))
            b2bf = e(nc.sbuf_tensor("b2bf", [1, DM], BF16))
            xnb = [e(nc.sbuf_tensor(f"xnb{i}", [128, KC * TB], BF16)) for i in range(2)]
            onb = [e(nc.sbuf_tensor(f"onb{i}", [128, H * TB], BF16)) for i in range(2)]
            xrs = [e(nc.sbuf_tensor(f"xrs{i}", [128, DM], F32)) for i in range(2)]
            outs = [e(nc.sbuf_tensor(f"outs{i}", [128, DM], F32)) for i in range(2)]
            hbuf = [e(nc.sbuf_tensor(f"hbuf{i}", [128, TB], BF16)) for i in range(3)]
            gsb = e(nc.sbuf_tensor("gsb", [128, 2 * DM], F32))
            sgt = e(nc.sbuf_tensor("sgt", [128, DM], F32))
            psH = [e(nc.psum_tensor(f"psH{i}", [128, TB], F32)) for i in range(2)]
            psO2 = [e(nc.psum_tensor(f"psO2{i}", [128, DM], F32)) for i in range(2)]
            psS = e(nc.psum_tensor("psS", [128, DM], F32))

            pl2 = Plan(sems, finalsA)
            BAR = ("bar", 1)
            pl2.counts["bar"] = 1

            pl2.add("gpsimd", lambda g: g.dma_start(
                out=wo[:, :].rearrange("p (he dm) -> p he dm", he=H),
                in_=woT.rearrange("(he p) dm -> p he dm", p=128)),
                inc="wB", dn=16, waits=[BAR])
            pl2.add("sync", lambda s: s.dma_start(out=b1s[:, :], in_=b1c),
                    inc="wBs", dn=16, waits=[BAR])
            pl2.add("gpsimd", lambda g: g.dma_start(out=b2bf[:, :], in_=b2r),
                    inc="wB", dn=16, waits=[BAR])
            EV_WB = [("wB", pl2.counts["wB"]), ("wBs", pl2.counts["wBs"])]
            ev_wsc = pl2.add("gpsimd", lambda g: g.tensor_tensor(
                wo[:, :], wo[:, :],
                nwb[:, 0:1].broadcast_to([128, H * DM]), ALU.mult),
                inc="g2", waits=[("wB", pl2.counts["wB"])])

            st_free = [None, None]
            w2rdy = [None] * FCH
            for c in range(FCH):
                srcp = w2T[c * 128:(c + 1) * 128, :]
                sl = c % 2
                ev_ld = pl2.add("sync", lambda s, sr=srcp, sl_=sl: s.dma_start(
                    out=stage[sl_][:, :], in_=sr), inc=f"su{sl}", dn=16,
                    waits=[st_free[sl], BAR])
                pl2.add("vector", lambda v, sl_=sl: v.tensor_single_scalar(
                    tmpb[:, :], stage[sl_][:, :], thrb[:, 3:4], ALU.is_lt),
                    inc="v2", waits=[ev_ld, BAR])
                ev_w = pl2.add("vector", lambda v, c_=c, sl_=sl: v.scalar_tensor_tensor(
                    w2t[:, c_ * DM:(c_ + 1) * DM], stage[sl_][:, :],
                    thrb[:, 2:3], tmpb[:, :], ALU.is_gt, ALU.subtract),
                    inc="v2")
                st_free[sl] = ev_w
                w2rdy[c] = ev_w

            xnb_free = [None, None]
            onb_free = [None, None]
            regs2 = {}

            def _mk_load2(eng):
                regs2[0] = nc.values_load(hsl[0:1, 0:1],
                                          engines=[mybir.EngineType.SP],
                                          skip_runtime_bounds_check=True)
                regs2[1] = nc.values_load(hsl[0:1, 1:2],
                                          engines=[mybir.EngineType.SP],
                                          skip_runtime_bounds_check=True)
                return eng.nop()
            pl2.add("sync", _mk_load2, waits=[BAR])

            def onb_fn(dst, s0, s1, sem):
                def f(eng):
                    with eng.If(regs2[0]):
                        eng.wait_ge(sems["cc1"], 1)
                        eng.dma_start(out=dst, in_=s0).then_inc(sems[sem], 16)
                    with eng.Else():
                        eng.wait_ge(sems["cc2"], 1)
                        eng.dma_start(out=dst, in_=s1).then_inc(sems[sem], 16)
                    return eng.nop()
                return f
            xr_free = [None, None]
            out_free = [None, None]
            psH_ev = [None, None]
            hb_free = [None, None, None]
            psS_ev = None
            psO2_ev = [None, None]
            gsb_last = [None, None]
            sgt_ev = None

            for i in range(NTB):
                t0 = i * TB
                par = i % 2
                src = xnT_d[:, t0:t0 + TB].rearrange("(c p) t -> p c t", p=128)
                dap = xnb[par][:, :].rearrange("p (c t) -> p c t", c=KC)
                ev_xnb = pl2.add("sync", lambda s, d=dap, sr=src: s.dma_start(
                    out=d, in_=sr), inc=f"bx{par}", dn=16,
                    waits=[xnb_free[par], None if i > 1 else ("fx0", finalsA["fx0"]),
                           None if i > 1 else ("fx1", finalsA["fx1"])])
                s0 = onT_g[0][:, t0:t0 + TB].rearrange("(h p) t -> p h t", p=128)
                s1 = onT_g[1][:, t0:t0 + TB].rearrange("(h p) t -> p h t", p=128)
                dap = onb[par][:, :].rearrange("p (h t) -> p h t", h=H)
                pl2.counts[f"bo{par}"] += 16
                ev_onb = (f"bo{par}", pl2.counts[f"bo{par}"])
                pl2.add("sync", onb_fn(dap, s0, s1, f"bo{par}"),
                        waits=[onb_free[par]])
                ev_xr = [None, None]
                for j2 in range(2):
                    src = xh_d[t0 + j2 * 128:t0 + j2 * 128 + 128, :]
                    ev_xr[j2] = pl2.add("gpsimd", lambda g, d=xrs[j2], sr=src: g.dma_start(
                        out=d[:, :], in_=sr), inc=f"br{j2}", dn=16,
                        waits=[xr_free[j2], BAR if i == 0 else None])

                ev_wo0 = None
                for he in range(H):
                    for n in range(2):
                        ev_wo0 = pl2.add("tensor", lambda t, he_=he, n_=n, p_=par: t.matmul(
                            psS[:, n_ * 512:(n_ + 1) * 512],
                            onb[p_][:, he_ * TB: he_ * TB + 128],
                            wo[:, he_ * DM + n_ * 512: he_ * DM + n_ * 512 + 512],
                            start=(he_ == 0), stop=(he_ == H - 1)),
                            inc=("t2" if (he == H - 1 and n == 1) else None),
                            waits=([ev_onb, psS_ev, ev_wsc] if he == 0 and n == 0 else []))

                gel_evs = {}

                def emit_ffn2(c2):
                    for j2 in range(2):
                        wts = [gel_evs[c2]]
                        if c2 == 0:
                            wts.append(psO2_ev[j2])
                            psO2_ev[j2] = None
                        if i == 0:
                            wts.append(w2rdy[c2])
                        for n in range(2):
                            pl2.add("tensor", lambda t, c_=c2, j_=j2, n_=n, sl_=c2 % 3: t.matmul(
                                psO2[j_][:, n_ * 512:(n_ + 1) * 512],
                                hbuf[sl_][:, j_ * 128:(j_ + 1) * 128],
                                w2t[:, c_ * DM + n_ * 512: c_ * DM + n_ * 512 + 512],
                                start=(c_ == 0), stop=False),
                                inc=("t2" if (j2 == 1 and n == 1) else None),
                                waits=(wts if n == 0 else []))

                for c in range(FCH):
                    sl = c % 3
                    ev_mm = None
                    for kc in range(KC):
                        ev_mm = pl2.add("tensor", lambda t, c_=c, kc_=kc, sl_=sl, p_=par: t.matmul(
                            psH[c_ % 2][:, :], w1t[:, kc_ * FF + c_ * 128: kc_ * FF + c_ * 128 + 128],
                            xnb[p_][:, kc_ * TB:(kc_ + 1) * TB],
                            start=(kc_ == 0), stop=(kc_ == KC - 1)),
                            inc=("t2" if kc == KC - 1 else None),
                            waits=([ev_xnb, psH_ev[c % 2], EV_WB] if kc == 0 else []))
                    if c == FCH - 1:
                        ev_ffn1_last = ev_mm
                    ev_gel = pl2.add("scalar", lambda s, c_=c, sl_=sl: s.activation(
                        hbuf[sl_][:, :], psH[c_ % 2][:, :], AF.Gelu, bias=b1s[:, c_:c_ + 1]),
                        inc="a2", waits=[ev_mm, hb_free[sl]])
                    gel_evs[c] = ev_gel
                    psH_ev[c % 2] = ev_gel
                    if c >= 2:
                        emit_ffn2(c - 2)
                        hb_free[(c - 2) % 3] = ("t2", pl2.counts["t2"])
                for c2 in (FCH - 2, FCH - 1):
                    emit_ffn2(c2)
                    hb_free[c2 % 3] = ("t2", pl2.counts["t2"])
                ev_ffn2 = None
                for j2 in range(2):
                    for n in range(2):
                        ev_ffn2 = pl2.add("tensor", lambda t, j_=j2, n_=n: t.matmul(
                            psO2[j_][:, n_ * 512:(n_ + 1) * 512], ones_rb[:, :],
                            b2bf[:, n_ * 512:n_ * 512 + 512], start=False, stop=True),
                            inc=("t2" if (j2 == 1 and n == 1) else None))
                ev_wo = [ev_wo0, None]
                for j2 in range(2):
                    ev_si = pl2.add("scalar", lambda s, j_=j2: s.activation(
                        gsb[:, j_ * DM:(j_ + 1) * DM], psO2[j_][:, :], AF.Silu),
                        inc="a2", waits=[ev_ffn2, gsb_last[j2]])
                    psO2_ev[j2] = ev_si
                    ev_sg = pl2.add("vector", lambda v, j_=j2: v.tensor_tensor(
                        sgt[:, :], psS[:, :], gsb[:, j_ * DM:(j_ + 1) * DM],
                        ALU.mult), inc="v2", waits=[ev_wo[j2], ev_si, sgt_ev])
                    gsb_last[j2] = ev_sg
                    if j2 == 0:
                        ev_w1 = None
                        for he in range(H):
                            for n in range(2):
                                ev_w1 = pl2.add("tensor", lambda t, he_=he, n_=n, p_=par: t.matmul(
                                    psS[:, n_ * 512:(n_ + 1) * 512],
                                    onb[p_][:, he_ * TB + 128: he_ * TB + 256],
                                    wo[:, he_ * DM + n_ * 512: he_ * DM + n_ * 512 + 512],
                                    start=(he_ == 0), stop=(he_ == H - 1)),
                                    inc=("t2" if (he == H - 1 and n == 1) else None),
                                    waits=([ev_sg] if he == 0 and n == 0 else []))
                        ev_wo[1] = ev_w1
                    else:
                        psS_ev = ev_sg
                    ev_add = pl2.add("gpsimd", lambda g, j_=j2: g.tensor_add(
                        outs[j_][:, :], sgt[:, :], xrs[j_][:, :]), inc="g2",
                        waits=[ev_sg, ev_xr[j2], out_free[j2]])
                    sgt_ev = ev_add
                    xr_free[j2] = ev_add
                    dstap = out_d[t0 + j2 * 128:t0 + j2 * 128 + 128, :]
                    ev_st = pl2.add("scalar", lambda s, d=dstap, sr=outs[j2]: s.dma_start(
                        out=d, in_=sr[:, :]), inc=f"bs{j2}", dn=16, waits=[ev_add])
                    out_free[j2] = ev_st
                xnb_free[par] = ev_ffn1_last
                onb_free[par] = ev_wo

            finalsB = dict(pl2.counts)

            with nc.Block() as blkB:
                @blkB.sync
                def _(engx):
                    pl2.realize("sync", engx)

                @blkB.gpsimd
                def _(engx):
                    pl2.realize("gpsimd", engx)
                    engx.wait_ge(sems["bs0"], finalsB["bs0"])
                    engx.wait_ge(sems["bs1"], finalsB["bs1"])

                @blkB.vector
                def _(engx):
                    pl2.realize("vector", engx)

                @blkB.scalar
                def _(engx):
                    pl2.realize("scalar", engx)

                @blkB.tensor
                def _(engx):
                    pl2.realize("tensor", engx)

    return nc


_NC_CACHE = None


def _get_nc():
    global _NC_CACHE
    if _NC_CACHE is None:
        _NC_CACHE = build()
    return _NC_CACHE


def kernel(L0_attn_hidden_states, L0_attn_Wq, L0_attn_Wk, L0_attn_Wv, L0_attn_Wo,
           L0_attn_norm_w, L0_ffn_W1, L0_ffn_b1, L0_ffn_W2, L0_ffn_b2):
    x = np.ascontiguousarray(np.asarray(L0_attn_hidden_states, dtype=np.float32))
    wq = np.ascontiguousarray(np.asarray(L0_attn_Wq, dtype=np.float32).T)
    wk = np.ascontiguousarray(np.asarray(L0_attn_Wk, dtype=np.float32).T)
    wv = np.ascontiguousarray(np.asarray(L0_attn_Wv, dtype=np.float32).T)
    wo = np.ascontiguousarray(np.asarray(L0_attn_Wo, dtype=np.float32).T)
    w1 = np.ascontiguousarray(np.asarray(L0_ffn_W1, dtype=np.float32).T)
    w2 = np.ascontiguousarray(np.asarray(L0_ffn_W2, dtype=np.float32).T)
    b1 = np.ascontiguousarray(np.asarray(L0_ffn_b1, dtype=np.float32).reshape(FCH, 128).T)
    b2 = np.ascontiguousarray(np.asarray(L0_ffn_b2, dtype=np.float32).reshape(1, DM))
    nw = np.ascontiguousarray(
        np.asarray(L0_attn_norm_w, dtype=np.float32).reshape(128, 1))

    nc = _get_nc()
    in_maps = []
    for c in range(8):
        b = c // 2
        half = c % 2
        es = slice(half * (HL * DH), (half + 1) * (HL * DH))
        in_maps.append({
            "x": np.ascontiguousarray(x[b]),
            "wqT": np.ascontiguousarray(wq[:, es]),
            "wkT": np.ascontiguousarray(wk[:, es]),
            "wvT": np.ascontiguousarray(wv[:, es]),
            "woT": wo, "w1T": w1, "w2T": w2, "b1c": b1, "b2r": b2, "nwb": nw,
            "xh": np.ascontiguousarray(x[b, half * (S // 2):(half + 1) * (S // 2)]),
            "hsl": np.array([[1 - half, half]], dtype=np.int32),
        })
    res = run_bass_kernel_spmd(nc, in_maps, core_ids=list(range(8)))
    out = np.empty((B, S, DM), dtype=np.float32)
    for b in range(B):
        out[b, 0:S // 2] = res.results[2 * b]["out"]
        out[b, S // 2:] = res.results[2 * b + 1]["out"]
    return out


# version 39
# speedup vs baseline: 1.0679x; 1.0679x over previous
import contextlib
import numpy as np
import ml_dtypes
import concourse.bass as bass
import concourse.mybir as mybir
from concourse.bass_utils import run_bass_kernel_spmd

F32 = mybir.dt.float32
BF16 = mybir.dt.bfloat16
AF = mybir.ActivationFunctionType
ALU = mybir.AluOpType
AX = mybir.AxisListType

B, S, DM, H, DH, FF = 4, 8192, 1024, 8, 128, 4096
KC = DM // 128
TILE = 512
NT = S // TILE
NB = TILE // 128
SW = 136
TB = 256
NTB = (S // 2) // TB
FCH = FF // 128
HL = H // 2
GROUPS = [[0, 1], [2, 3], [4, 5], [6, 7]]
LN_EPS = 1e-5
RMS_EPS = 1e-5
INV_NW = 1.0 / (FF * DM)


class Plan:

    ENGS = ("sync", "gpsimd", "vector", "scalar", "tensor")

    def __init__(self, sems, counts=None):
        self.sems = sems
        self.counts = dict(counts) if counts else {k: 0 for k in sems}
        self.ops = {e: [] for e in self.ENGS}

    def add(self, eng, fn, inc=None, dn=1, waits=()):
        ws = []
        for w in waits:
            if w is None:
                continue
            if isinstance(w, list):
                ws.extend(x for x in w if x is not None)
            else:
                ws.append(w)
        for (s, v) in ws:
            assert v <= self.counts[s], f"wait on future {s}={v}>{self.counts[s]}"
        ev = None
        if inc is not None:
            self.counts[inc] += dn
            ev = (inc, self.counts[inc])
        self.ops[eng].append((fn, inc, dn, tuple(ws)))
        return ev

    def realize(self, eng_name, eng):
        seen = {k: 0 for k in self.sems}
        for fn, inc, dn, ws in self.ops[eng_name]:
            for (s, v) in ws:
                if v > seen[s]:
                    eng.wait_ge(self.sems[s], v)
                    seen[s] = v
            inst = fn(eng)
            if inc is not None:
                inst.then_inc(self.sems[inc], dn)


def build():
    nc = bass.Bass("TRN2", target_bir_lowering=False, detect_race_conditions=False)
    x_d = nc.dram_tensor("x", [S, DM], F32, kind="ExternalInput").ap()
    wqT = nc.dram_tensor("wqT", [DM, HL * DH], F32, kind="ExternalInput").ap()
    wkT = nc.dram_tensor("wkT", [DM, HL * DH], F32, kind="ExternalInput").ap()
    wvT = nc.dram_tensor("wvT", [DM, HL * DH], F32, kind="ExternalInput").ap()
    woT = nc.dram_tensor("woT", [DM, DM], F32, kind="ExternalInput").ap()
    w1T = nc.dram_tensor("w1T", [DM, FF], F32, kind="ExternalInput").ap()
    w2T = nc.dram_tensor("w2T", [FF, DM], F32, kind="ExternalInput").ap()
    b1c = nc.dram_tensor("b1c", [128, FCH], F32, kind="ExternalInput").ap()
    b2r = nc.dram_tensor("b2r", [1, DM], F32, kind="ExternalInput").ap()
    nwb_d = nc.dram_tensor("nwb", [128, 1], F32, kind="ExternalInput").ap()
    xh_d = nc.dram_tensor("xh", [S // 2, DM], F32, kind="ExternalInput").ap()
    hsl_d = nc.dram_tensor("hsl", [1, 2], mybir.dt.int32, kind="ExternalInput").ap()
    out_d = nc.dram_tensor("out", [S // 2, DM], F32, kind="ExternalOutput").ap()

    id_np = np.eye(128, dtype=ml_dtypes.bfloat16)
    maskT_np = np.tile(np.triu(np.ones((128, 128), dtype=np.float32)), (1, 4)).astype(ml_dtypes.bfloat16)
    idb_d = nc.inline_tensor(id_np, "idb_c").ap()
    maskT_d = nc.inline_tensor(maskT_np, "maskT_c").ap()
    ones_cb_d = nc.inline_tensor(np.ones((128, 2), dtype=ml_dtypes.bfloat16), "ones_cb_c").ap()
    ones_rb_d = nc.inline_tensor(np.ones((1, 128), dtype=ml_dtypes.bfloat16), "ones_rb_c").ap()
    ones_cf_d = nc.inline_tensor(np.ones((128, 1), dtype=np.float32), "ones_cf_c").ap()
    ones_rf_d = nc.inline_tensor(np.ones((1, 128), dtype=np.float32), "ones_rf_c").ap()

    onT_snd = [nc.dram_tensor(f"onT_snd{r}", [HL * DH, S // 2], BF16, kind="Internal").ap()
               for r in range(2)]
    onT_g = [nc.dram_tensor(f"onT_g{r}", [DM, S // 2], BF16, kind="Internal").ap()
             for r in range(2)]
    xnT_d = nc.dram_tensor("xnT_s", [DM, S // 2], BF16, kind="Internal").ap()

    sem_names = ["c0", "cg", "x0", "x1", "fx0", "fx1", "fo0", "fo1",
                 "t", "v", "a", "g", "bar", "wB", "wBs", "bx0", "bx1", "bo0", "bo1",
                 "br0", "br1", "bs0", "bs1", "st0", "st1", "t2", "v2", "a2", "g2",
                 "cc1", "cc2", "su0", "su1"]

    with contextlib.ExitStack() as octx:
        oe = octx.enter_context
        sems = {n: oe(nc.semaphore(n)) for n in sem_names}

        idb = oe(nc.sbuf_tensor("idb", [128, 128], BF16))
        maskT = oe(nc.sbuf_tensor("maskT", [128, 512], BF16))
        ones_cb = oe(nc.sbuf_tensor("ones_cb", [128, 2], BF16))
        ones_rb = oe(nc.sbuf_tensor("ones_rb", [1, 128], BF16))
        ones_cf = oe(nc.sbuf_tensor("ones_cf", [128, 1], F32))
        ones_rf = oe(nc.sbuf_tensor("ones_rf", [1, 128], F32))
        nwb = oe(nc.sbuf_tensor("nwb_sb", [128, 1], F32))
        hsl = oe(nc.sbuf_tensor("hsl_sb", [1, 2], mybir.dt.int32))
        w1t = oe(nc.sbuf_tensor("w1t", [128, KC * FF], BF16))
        stage = [oe(nc.sbuf_tensor(f"stg{i}", [128, 1024], F32)) for i in range(2)]
        tmpb = oe(nc.sbuf_tensor("tmpb", [128, 1024], BF16))
        accr = oe(nc.sbuf_tensor("accr", [128, 68], F32))
        thr4 = oe(nc.sbuf_tensor("thr4", [1, 4], F32))
        sc12 = oe(nc.sbuf_tensor("sc12", [1, 2], F32))
        totc = oe(nc.sbuf_tensor("totc", [1, 4], F32))
        eps2 = oe(nc.sbuf_tensor("eps2", [1, 2], F32))
        thrb = oe(nc.sbuf_tensor("thrb", [128, 4], F32))

        with contextlib.ExitStack() as actx:
            e = actx.enter_context
            wq = e(nc.sbuf_tensor("wq", [128, KC * HL * DH], BF16))
            wk = e(nc.sbuf_tensor("wk", [128, KC * HL * DH], BF16))
            wv = e(nc.sbuf_tensor("wv", [128, KC * HL * DH], BF16))
            Sf = e(nc.sbuf_tensor("Sf", [128, HL * SW], F32))
            Sbf = [e(nc.sbuf_tensor(f"Sbf{i}", [128, HL * SW], BF16)) for i in range(2)]
            dbf_s = e(nc.sbuf_tensor("dbf_s", [128, HL * 128], BF16))
            dbf_z = e(nc.sbuf_tensor("dbf_z", [128, HL * 2], BF16))
            xt = [e(nc.sbuf_tensor(f"xt{i}", [128, NB * DM], F32)) for i in range(2)]
            xnf = [e(nc.sbuf_tensor(f"xnf{i}", [128, KC * TILE], BF16)) for i in range(2)]
            qf = e(nc.sbuf_tensor("qf", [128, HL * TILE], BF16))
            kf = e(nc.sbuf_tensor("kf", [128, HL * TILE], BF16))
            vt = e(nc.sbuf_tensor("vt", [128, NB * HL * DH], BF16))
            ktb = e(nc.sbuf_tensor("ktb", [128, NB * HL * 128], BF16))
            onsb = e(nc.sbuf_tensor("onsb", [128, NB * HL * 128], BF16))
            osb = e(nc.sbuf_tensor("osb", [128, NB * 512], F32))
            onT = [e(nc.sbuf_tensor(f"onT{i}", [128, HL * TILE], BF16)) for i in range(2)]
            xn_tm = [e(nc.sbuf_tensor(f"xn_tm{i}", [128, DM], BF16)) for i in range(2)]
            sq_scr = e(nc.sbuf_tensor("sq_scr", [128, DM], BF16))
            id_scr = e(nc.sbuf_tensor("id_scr", [128, DM], BF16))
            sc_sb = e(nc.sbuf_tensor("sc_sb", [128, 512], BF16))
            stats = e(nc.sbuf_tensor("stats", [128, 64], F32))
            atts = e(nc.sbuf_tensor("atts", [128, 320], F32))
            ones16 = e(nc.sbuf_tensor("ones16", [128, 32], F32))
            epsc = e(nc.sbuf_tensor("epsc", [128, 1], F32))
            psA = [e(nc.psum_tensor(f"psA{i}", [128, 512], F32)) for i in range(2)]
            psO = [e(nc.psum_tensor(f"psO{i}", [128, 512], F32)) for i in range(2)]
            psDen = e(nc.psum_tensor("psDen", [128, 16], F32))
            psSd = e(nc.psum_tensor("psSd", [128, 512], F32))
            psT2 = [e(nc.psum_tensor(f"psT{i}", [128, 512], BF16)) for i in range(2)]

            pl = Plan(sems)

            for dst, src in ((idb, idb_d), (maskT, maskT_d), (ones_cb, ones_cb_d),
                             (ones_rb, ones_rb_d), (ones_cf, ones_cf_d),
                             (ones_rf, ones_rf_d), (nwb, nwb_d), (hsl, hsl_d)):
                pl.add("sync", lambda s, d=dst, sr=src: s.dma_start(out=d[:, :], in_=sr),
                       inc="c0", dn=16)
            for dst, src in ((wq, wqT), (wk, wkT), (wv, wvT)):
                ap = src.rearrange("(kc p) e -> p kc e", p=128)
                dap = dst[:, :].rearrange("p (kc e) -> p kc e", kc=KC)

                pl.add("gpsimd", lambda g, d=dap, sr=ap: g.dma_start(out=d, in_=sr),
                       inc="cg", dn=16)
            EV_C0 = [("c0", pl.counts["c0"]), ("cg", pl.counts["cg"])]

            pl.add("vector", lambda v: v.memset(ones16[:, :], 1.0), inc="v")
            pl.add("vector", lambda v: v.memset(epsc[:, :], LN_EPS), inc="v")
            regs = {}

            def _mk_load(engname, enum):
                def f(eng):
                    regs[(engname, 0)] = nc.values_load(
                        hsl[0:1, 0:1], engines=[enum], skip_runtime_bounds_check=True)
                    regs[(engname, 1)] = nc.values_load(
                        hsl[0:1, 1:2], engines=[enum], skip_runtime_bounds_check=True)
                    return eng.nop()
                return f
            pl.add("scalar", _mk_load("scalar", mybir.EngineType.Activation),
                   waits=[("c0", pl.counts["c0"])])
            pl.add("sync", _mk_load("sync", mybir.EngineType.SP),
                   waits=[("c0", pl.counts["c0"])])

            def spill_fn(engname, dst, srcap, want, sem):
                def f(eng):
                    with eng.If(regs[(engname, want)]):
                        eng.dma_start(out=dst, in_=srcap).then_inc(sems[sem], 16)
                    with eng.Else():
                        eng.sem_inc(sems[sem], 16)
                    return eng.nop()
                return f
            pl.add("vector", lambda v: v.memset(atts[:, :], 0.0), inc="v")
            pl.add("vector", lambda v: v.memset(eps2[:, :], 1e-8), inc="v")
            pl.add("vector", lambda v: v.memset(Sf[:, :], 0.0), inc="v")
            pl.add("vector", lambda v: v.memset(Sbf[0][:, :], 0.0), inc="v")
            pl.add("vector", lambda v: v.memset(Sbf[1][:, :], 0.0), inc="v")
            pl.add("vector", lambda v: v.memset(dbf_s[:, :], 0.0), inc="v")
            ev_sb0 = pl.add("vector", lambda v: v.memset(dbf_z[:, :], 0.0), inc="v")

            tern_state = {"stf": [None, None], "ld": 0}

            def tern_stage_load(src_ap, sl):
                w = [tern_state["stf"][sl], EV_C0]
                return pl.add("scalar", lambda s, sr=src_ap, sl_=sl: s.dma_start(
                    out=stage[sl_][:, :], in_=sr), inc=f"st{sl}", dn=16, waits=w)

            def emit_scale_chunk(c):
                if c < 32:
                    srcp = w1T[(c % 8) * 128:(c % 8) * 128 + 128,
                               (c // 8) * 1024:(c // 8) * 1024 + 1024]
                else:
                    srcp = w2T[(c - 32) * 128:(c - 31) * 128, :]
                sl = c % 2
                ev_ld = tern_stage_load(srcp, sl)
                ev_r = pl.add("vector", lambda v, c_=c, sl_=sl: v.tensor_reduce(
                    accr[:, c_:c_ + 1], stage[sl_][:, :], AX.X, ALU.add,
                    apply_absolute_value=True), inc="v", waits=[ev_ld])
                tern_state["stf"][sl] = ev_r

            w1rdy = [None] * 4

            def emit_w1_chunk(ct, ev_thrb_):
                fi, kc = ct // 8, ct % 8
                srcp = w1T[kc * 128:(kc + 1) * 128, fi * 1024:(fi + 1) * 1024]
                sl = ct % 2
                ev_ld = tern_stage_load(srcp, sl)
                pl.add("vector", lambda v, sl_=sl: v.tensor_single_scalar(
                    tmpb[:, :], stage[sl_][:, :], thrb[:, 1:2], ALU.is_lt),
                    inc="v", waits=[ev_ld, ev_thrb_])
                ev_w = pl.add("vector", lambda v, kc_=kc, fi_=fi, sl_=sl: v.scalar_tensor_tensor(
                    w1t[:, kc_ * FF + fi_ * 1024: kc_ * FF + fi_ * 1024 + 1024],
                    stage[sl_][:, :], thrb[:, 0:1], tmpb[:, :],
                    ALU.is_gt, ALU.subtract), inc="v")
                tern_state["stf"][sl] = ev_w
                if kc == 7:
                    w1rdy[fi] = ev_w
                return ev_w

            xt_free = [None, None]
            xnf_free = [None, None]
            onT_free = [None, None]
            last_sbf = ev_sb0
            sbf_ev = [ev_sb0, ev_sb0]
            dbf_ev = ev_sb0
            psA_ev = [None, None]
            psO_ev = [None, None]
            psDen_ev = None
            psSd_ev = None
            psT_free = [None, None]
            last_stats_w = None
            last_var_g = None
            last_ms_ev = None
            last_dn_ev = None
            last_ep_act = None
            last_ab = None
            osb_war = None
            last_xntp = None
            blkctr = 0

            def emit_xload(ti):
                par2 = ti % 2
                evs = []
                for b in range(NB):
                    srcx = x_d[ti * TILE + b * 128: ti * TILE + (b + 1) * 128, :]
                    dapx = xt[par2][:, b * DM:(b + 1) * DM]
                    ev = pl.add("sync", lambda s, d=dapx, sr=srcx: s.dma_start(
                        out=d, in_=sr), inc=f"x{par2}", dn=16,
                        waits=[xt_free[par2]] if b == 0 else [])
                    evs.append(ev)
                xload_ev[par2] = evs

            def emit_ln_stats(ti, j):
                nonlocal last_stats_w, last_var_g
                par2 = ti % 2
                B0 = 32 * par2
                xblk = xt[par2][:, j * DM:(j + 1) * DM]
                pl.add("scalar", lambda s, xb=xblk, j_=j, b_=B0: s.activation(
                    sq_scr[:, :], xb, AF.Square, scale=1.0 / 32.0,
                    accum_out=stats[:, b_ + j_:b_ + j_ + 1]), inc="a",
                    waits=[xload_ev[par2][j]] + ([last_stats_w, last_var_g] if j == 0 else []))
                lnp[(ti, j)] = pl.add("scalar", lambda s, xb=xblk, j_=j, b_=B0: s.activation(
                    id_scr[:, :], xb, AF.Identity, scale=1.0 / DM,
                    accum_out=stats[:, b_ + 4 + j_:b_ + 5 + j_]), inc="a")

            def emit_ln_chain_j(ti, j):
                nonlocal last_var_g
                par2 = ti % 2
                B0 = 32 * par2
                ev_sqid = lnp.pop((ti, j))
                ev_m2 = pl.add("vector", lambda v, b_=B0, j_=j: v.tensor_tensor(
                    stats[:, b_ + 8 + j_:b_ + 9 + j_], stats[:, b_ + 4 + j_:b_ + 5 + j_],
                    stats[:, b_ + 4 + j_:b_ + 5 + j_], ALU.mult),
                    inc="v", waits=[ev_sqid] + ([last_var_g] if j == 0 else []))
                ev_var = pl.add("gpsimd", lambda g, b_=B0, j_=j: g.tensor_tensor(
                    stats[:, b_ + 12 + j_:b_ + 13 + j_], stats[:, b_ + j_:b_ + 1 + j_],
                    stats[:, b_ + 8 + j_:b_ + 9 + j_], ALU.subtract),
                    inc="g", waits=[ev_m2])
                if j == NB - 1:
                    last_var_g = ev_var
                ev_sdl = pl.add("scalar", lambda s, b_=B0, j_=j: s.activation(
                    stats[:, b_ + 16 + j_:b_ + 17 + j_],
                    stats[:, b_ + 12 + j_:b_ + 13 + j_],
                    AF.Sqrt, bias=epsc[:, 0:1]), inc="a", waits=[ev_var])
                ev_rr = pl.add("vector", lambda v, b_=B0, j_=j: v.reciprocal(
                    stats[:, b_ + 20 + j_:b_ + 21 + j_],
                    stats[:, b_ + 16 + j_:b_ + 17 + j_]),
                    inc="v", waits=[ev_sdl])
                ev_rs = pl.add("scalar", lambda s, b_=B0, j_=j: s.copy(
                    stats[:, b_ + 24 + j_:b_ + 25 + j_],
                    stats[:, b_ + 20 + j_:b_ + 21 + j_]),
                    inc="a", waits=[ev_rr])
                lnp[(ti, "rs", j)] = ev_rs

            def emit_ln_xn_j(ti, j):
                nonlocal last_stats_w, xnf_ready, xnf_free
                par2 = ti % 2
                B0 = 32 * par2
                ev_rs = lnp.pop((ti, "rs", j))
                bp2 = j % 2
                xblk = xt[par2][:, j * DM:(j + 1) * DM]
                ev_xn = pl.add("vector", lambda v, xb=xblk, j_=j, b_=B0, p_=bp2: v.tensor_scalar(
                    xn_tm[p_][:, :], xb, stats[:, b_ + 4 + j_:b_ + 5 + j_],
                    stats[:, b_ + 24 + j_:b_ + 25 + j_],
                    ALU.subtract, ALU.mult), inc="v",
                    waits=[ev_rs, last_xn_tp[bp2]])
                ev_last_evac = None
                for q in range(2):
                    sl = q
                    ev_tp = None
                    for c4 in range(4):
                        c = q * 4 + c4
                        ev_tp = pl.add("tensor", lambda t, c_=c, c4_=c4, sl_=sl, p_=bp2: t.transpose(
                            psT2[sl_][:, c4_ * 128:(c4_ + 1) * 128],
                            xn_tm[p_][:, c_ * 128:(c_ + 1) * 128], idb[:, :]),
                            inc=("t" if c4 == 3 else None),
                            waits=([ev_xn, psT_free[sl]] if c4 == 0 else []))
                    ev_ev = pl.add("scalar", lambda s, q_=q, j_=j, sl_=sl, p_=par2: s.copy(
                        xnf[p_][:, :].rearrange("p (c t) -> p c t", c=KC)[:, q_ * 4:q_ * 4 + 4, j_ * 128:j_ * 128 + 128],
                        psT2[sl_][:, :].rearrange("p (c t) -> p c t", c=4)),
                        inc="a", waits=[ev_tp, xnf_free[par2] if j == 0 and q == 0 else None])
                    psT_free[sl] = ev_ev
                    ev_last_evac = ev_ev
                    last_xn_tp[bp2] = ev_tp
                if j < NB - 1:
                    return
                lc = (ti * TILE) % (S // 2)
                want = 0 if ti * TILE < S // 2 else 1
                xnf_free[par2] = None
                last_stats_w = ev_xn
                xt_free[par2] = [ev_xn]
                xnf_ready[par2] = ev_last_evac
                dstap = xnT_d[:, lc:lc + TILE].rearrange("(c p) t -> p c t", p=128)
                srcap = xnf[par2][:, :].rearrange("p (c t) -> p c t", c=KC)
                pl.counts[f"fx{par2}"] += 16
                ev_fx = (f"fx{par2}", pl.counts[f"fx{par2}"])
                pl.add("sync", spill_fn("sync", dstap, srcap, want, f"fx{par2}"),
                       waits=[ev_last_evac])
                xnf_free[par2] = ev_fx

            xload_ev = [None, None]
            xnf_ready = [None, None]
            last_xn_tp = [None, None]
            ring_unused = None
            lnp = {}
            emit_xload(0)
            for b in range(NB):
                emit_ln_stats(0, b)
            for b in range(NB):
                emit_ln_chain_j(0, b)
            for b in range(NB):
                emit_ln_xn_j(0, b)

            for i in range(NT):
                t0 = i * TILE
                par = i % 2
                lc = t0 % (S // 2)
                want = 0 if t0 < S // 2 else 1
                if i + 1 < NT:
                    emit_xload(i + 1)
                ev_xnf_ready = xnf_ready[par]

                k_evac = [None] * HL
                v_evac = [None] * NB
                WH = HL * DH
                for wi, (wmat, dstf) in enumerate(((wq, qf), (wk, kf))):
                    for h in range(HL):
                        sl = h % 2
                        ev_mm = None
                        for kc in range(KC):
                            ev_mm = pl.add("tensor", lambda t, h_=h, kc_=kc, sl_=sl, wm=wmat, p_=par: t.matmul(
                                psA[sl_][:, 0:512],
                                wm[:, kc_ * WH + h_ * 128: kc_ * WH + h_ * 128 + 128],
                                xnf[p_][:, kc_ * TILE:(kc_ + 1) * TILE],
                                start=(kc_ == 0), stop=(kc_ == KC - 1)),
                                inc=("t" if kc == KC - 1 else None),
                                waits=([ev_xnf_ready, EV_C0, psA_ev[sl]] if kc == 0 else []))
                        ev_ev = pl.add("vector", lambda v, h_=h, sl_=sl, df=dstf: v.tensor_copy(
                            df[:, h_ * TILE:(h_ + 1) * TILE], psA[sl_][:, 0:512]),
                            inc="v", waits=[ev_mm])
                        psA_ev[sl] = ev_ev
                        if wi == 1:
                            k_evac[h] = ev_ev
                for j in range(NB):
                    sl = j % 2
                    ev_mm = None
                    for kc in range(KC):
                        ev_mm = pl.add("tensor", lambda t, j_=j, kc_=kc, sl_=sl, p_=par: t.matmul(
                            psA[sl_][:, 0:512],
                            xnf[p_][:, kc_ * TILE + j_ * 128: kc_ * TILE + j_ * 128 + 128],
                            wv[:, kc_ * WH: kc_ * WH + 512],
                            start=(kc_ == 0), stop=(kc_ == KC - 1)),
                            inc=("t" if kc == KC - 1 else None),
                            waits=([ev_xnf_ready, EV_C0, psA_ev[sl]] if kc == 0 else []))
                    ev_ev = pl.add("vector", lambda v, j_=j, sl_=sl: v.tensor_copy(
                        vt[:, j_ * 512: j_ * 512 + 512],
                        psA[sl_][:, 0:512]), inc="v", waits=[ev_mm])
                    psA_ev[sl] = ev_ev
                    v_evac[j] = ev_ev

                ktb_ev = [None] * NB
                for j in range(NB):
                    sl = j % 2
                    ev_ktp = None
                    for h in range(HL):
                        ksl = kf[:, h * TILE + j * 128: h * TILE + j * 128 + 128]
                        ev_ktp = pl.add("tensor", lambda t, ks=ksl, h_=h, sl_=sl: t.transpose(
                            psT2[sl_][:, h_ * 128:(h_ + 1) * 128], ks, idb[:, :]),
                            inc=("t" if h == HL - 1 else None),
                            waits=([k_evac[HL - 1], psT_free[sl]] if h == 0 else []))
                    ev_kte = pl.add("scalar", lambda s, j_=j, sl_=sl: s.copy(
                        ktb[:, j_ * 512:(j_ + 1) * 512],
                        psT2[sl_][:, :]), inc="a", waits=[ev_ktp])
                    psT_free[sl] = ev_kte
                    ktb_ev[j] = ev_kte

                for j in range(NB):
                    p = (blkctr >> 1) & 1
                    scb = 0
                    ev_sc = None
                    for h in range(HL):
                        qsl = qf[:, h * TILE + j * 128: h * TILE + j * 128 + 128]
                        ksl = kf[:, h * TILE + j * 128: h * TILE + j * 128 + 128]
                        ev_sc = pl.add("tensor", lambda t, ks=ksl, qs=qsl, h_=h, b_=scb: t.matmul(
                            psA[b_][:, h_ * 128:h_ * 128 + 128], ks, qs,
                            start=(h_ == 0), stop=(h_ == HL - 1)),
                            inc=("t" if h == HL - 1 else None),
                            waits=([psA_ev[scb]] if h == 0 else []))
                    psA_ev[scb] = None
                    ev_me = pl.add("vector", lambda v, b_=scb: v.tensor_tensor(
                        sc_sb[:, :], psA[b_][:, 0:512], maskT[:, :], ALU.mult),
                        inc="v", waits=[ev_sc])
                    psA_ev[scb] = ev_me
                    for h in range(HL):
                        qsl = qf[:, h * TILE + j * 128: h * TILE + j * 128 + 128]
                        wts = []
                        if h == 0:
                            wts = [sbf_ev[p], dbf_ev, psO_ev[p]]
                            psO_ev[p] = None
                        pl.add("tensor", lambda t, qs=qsl, h_=h, p_=p: t.matmul(
                            psO[p_][:, h_ * 128:h_ * 128 + 128], qs,
                            Sbf[p_][:, h_ * SW:h_ * SW + 128], start=(h_ == 0), stop=False),
                            waits=wts)
                        pl.add("tensor", lambda t, qs=qsl, h_=h, p_=p: t.matmul(
                            psO[p_][:, h_ * 128:h_ * 128 + 128], qs,
                            dbf_s[:, h_ * 128:h_ * 128 + 128], start=False, stop=False))
                    ev_o2 = None
                    for h in range(HL):
                        vsl = vt[:, j * 512 + h * 128: j * 512 + h * 128 + 128]
                        ev_o2 = pl.add("tensor", lambda t, h_=h, vs=vsl, p_=p: t.matmul(
                            psO[p_][:, h_ * 128:h_ * 128 + 128],
                            sc_sb[:, h_ * 128:h_ * 128 + 128], vs,
                            start=False, stop=(h_ == HL - 1)),
                            inc=("t" if h == HL - 1 else None),
                            waits=([ev_me, v_evac[j]] if h == 0 else []))
                    for h in range(HL):
                        vsl = vt[:, j * 512 + h * 128: j * 512 + h * 128 + 128]
                        pl.add("tensor", lambda t, j_=j, h_=h, vs=vsl: t.matmul(
                            psSd[:, h_ * 128:h_ * 128 + 128],
                            ktb[:, (j_ * HL + h_) * 128:(j_ * HL + h_) * 128 + 128], vs,
                            start=(h_ == 0), stop=(h_ == HL - 1)),
                            waits=([ktb_ev[j], psSd_ev] if h == 0 else []))
                    for h in range(HL):
                        qsl = qf[:, h * TILE + j * 128: h * TILE + j * 128 + 128]
                        pl.add("tensor", lambda t, qs=qsl, h_=h, p_=p: t.matmul(
                            psDen[:, 2 * h_:2 * h_ + 2], qs,
                            Sbf[p_][:, h_ * SW + 128:h_ * SW + 130],
                            start=(h_ == 0), stop=False),
                            waits=([psDen_ev] if h == 0 else []))
                        pl.add("tensor", lambda t, qs=qsl, h_=h: t.matmul(
                            psDen[:, 2 * h_:2 * h_ + 2], qs,
                            dbf_z[:, 2 * h_:2 * h_ + 2],
                            start=False, stop=False))
                    psDen_ev = None
                    for h in range(HL):
                        pl.add("tensor", lambda t, h_=h: t.matmul(
                            psDen[:, 2 * h_:2 * h_ + 1],
                            sc_sb[:, h_ * 128:h_ * 128 + 128], ones_cb[:, 0:1],
                            start=False, stop=False))
                    ev_z = None
                    for h in range(HL):
                        ev_z = pl.add("tensor", lambda t, j_=j, h_=h: t.matmul(
                            psDen[:, 8 + 2 * h_:10 + 2 * h_],
                            ktb[:, (j_ * HL + h_) * 128:(j_ * HL + h_) * 128 + 128],
                            ones_cb[:, 0:2], start=False, stop=(h_ == HL - 1)),
                            inc=("t" if h == HL - 1 else None))
                    ev_dbf1 = pl.add("vector", lambda v: v.tensor_copy(
                        dbf_s[:, :], psSd[:, 0:512]), inc="v", waits=[ev_z])
                    ev_dbf2 = pl.add("vector", lambda v: v.tensor_copy(
                        dbf_z[:, :], psDen[:, 8:16]), inc="v")
                    dbf_ev = ev_dbf2
                    sfv = Sf[:, 0:HL * SW].rearrange("p (h w) -> p h w", w=SW)
                    ev_sa1 = pl.add("vector", lambda v, s_=sfv: v.tensor_tensor(
                        s_[:, :, 0:128], s_[:, :, 0:128],
                        psSd[:, 0:512].rearrange("p (h d) -> p h d", d=128), ALU.add),
                        inc="v", waits=[ev_z, last_sbf])
                    ev_sa2 = pl.add("vector", lambda v, s_=sfv: v.tensor_tensor(
                        s_[:, :, 128:130], s_[:, :, 128:130],
                        psDen[:, 8:16].rearrange("p (h d) -> p h d", d=2), ALU.add),
                        inc="v")
                    psSd_ev = [ev_sa1, ev_dbf1]
                    ev_sbf = pl.add("gpsimd", lambda g, p_=p: g.tensor_copy(
                        Sbf[p_][:, 0:HL * SW], Sf[:, 0:HL * SW]), inc="g",
                        waits=[ev_sa2])
                    last_sbf = ev_sbf
                    sbf_ev[p] = ev_sbf
                    ev_oev = pl.add("scalar", lambda s, j_=j, p_=p: s.copy(
                        osb[:, j_ * 512:(j_ + 1) * 512], psO[p_][:, 0:512]),
                        inc="a", waits=[ev_o2, osb_war if j == 0 else None])
                    psO_ev[p] = ev_oev
                    for h in range(HL):
                        pl.add("scalar", lambda s, j_=j, h_=h: s.activation(
                            sq_scr[:, 0:128], osb[:, j_ * 512 + h_ * 128: j_ * 512 + h_ * 128 + 128],
                            AF.Square, scale=0.08838834764831845,
                            accum_out=atts[:, j_ * 8 + 2 * h_:j_ * 8 + 2 * h_ + 1]),
                            inc=None, waits=[last_ms_ev if j == 0 and h == 0 else None])
                    ev_ab = pl.add("scalar", lambda s, j_=j: s.activation(
                        atts[:, 64 + j_ * 8:72 + j_ * 8], psDen[:, 0:8], AF.Abs), inc="a",
                        waits=[last_dn_ev if j == 0 else None])
                    psDen_ev = [ev_ab, ev_sa2, ev_dbf2]
                    last_ab = ev_ab
                    blkctr += 2
                    if i + 1 < NT:
                        if j >= 1:
                            emit_ln_chain_j(i + 1, j - 1)
                        emit_ln_stats(i + 1, j)
                        if j >= 2:
                            emit_ln_xn_j(i + 1, j - 2)

                if i + 1 < NT:
                    emit_ln_chain_j(i + 1, NB - 1)
                    emit_ln_xn_j(i + 1, 2)

                ev_dn = pl.add("vector", lambda v: v.tensor_tensor(
                    atts[:, 128:160], atts[:, 64:96], ones16[:, :], ALU.max),
                    inc="v", waits=[last_ab, last_ep_act])
                last_dn_ev = ev_dn
                ev_ed = pl.add("scalar", lambda s: s.activation(
                    atts[:, 160:192], atts[:, 128:160], AF.Square,
                    scale=0.0031622776601683794), inc="a", waits=[ev_dn])
                ev_ms = pl.add("vector", lambda v: v.tensor_tensor(
                    atts[:, 192:224], atts[:, 160:192], atts[:, 0:32], ALU.add),
                    inc="v", waits=[ev_ed])
                last_ms_ev = ev_ms
                ev_sdm = pl.add("scalar", lambda s: s.activation(
                    atts[:, 224:256], atts[:, 192:224], AF.Sqrt), inc="a",
                    waits=[ev_ms])
                ev_ar = pl.add("vector", lambda v: v.reciprocal(
                    atts[:, 256:288], atts[:, 224:256]), inc="v", waits=[ev_sdm])
                ev_al = pl.add("scalar", lambda s: s.copy(
                    atts[:, 288:304].unsqueeze(-1),
                    atts[:, 256:288].rearrange("p (n two) -> p n two", two=2)[:, :, 0:1]),
                    inc="a", waits=[ev_ar])
                last_ep_act = ev_al
                ev_on_last = None
                on_evs = []
                for j in range(NB):
                    ev_on = None
                    for h in range(HL):
                        col = 288 + j * 4 + h
                        ev_on = pl.add("vector", lambda v, j_=j, h_=h, c_=col: v.tensor_scalar(
                            onsb[:, j_ * 512 + h_ * 128: j_ * 512 + h_ * 128 + 128],
                            osb[:, j_ * 512 + h_ * 128: j_ * 512 + h_ * 128 + 128],
                            atts[:, c_:c_ + 1], None, ALU.mult, ALU.bypass),
                            inc="v", waits=([ev_al] if j == 0 and h == 0 else []))
                    on_evs.append(ev_on)
                    ev_on_last = ev_on
                if i + 1 < NT:
                    emit_ln_xn_j(i + 1, 3)
                for j in range(NB):
                    sl = j % 2
                    ev_otp = None
                    for h in range(HL):
                        ev_otp = pl.add("tensor", lambda t, j_=j, h_=h, sl_=sl: t.transpose(
                            psT2[sl_][:, h_ * 128:(h_ + 1) * 128],
                            onsb[:, j_ * 512 + h_ * 128: j_ * 512 + h_ * 128 + 128],
                            idb[:, :]), inc=("t" if h == HL - 1 else None),
                            waits=([on_evs[j], psT_free[sl]] if h == 0 else []))
                    ev_oevT = pl.add("scalar", lambda s, sl_=sl, j_=j, p_=par: s.copy(
                        onT[p_][:, :].rearrange("p (h t) -> p h t", h=HL)[:, :, j_ * 128:j_ * 128 + 128],
                        psT2[sl_][:, :].rearrange("p (h t) -> p h t", h=HL)),
                        inc="a", waits=[ev_otp, onT_free[par] if j == 0 else None])
                    psT_free[sl] = ev_oevT
                onT_free[par] = None
                osb_war = ev_on_last
                ev_onT_tile = ev_oevT

                dstap = onT_snd[want][:, lc:lc + TILE].rearrange("(h p) t -> p h t", p=128)
                srcap = onT[par][:, :].rearrange("p (h t) -> p h t", h=HL)
                ev_fo = pl.add("scalar", lambda s, d=dstap, sr=srcap: s.dma_start(
                    out=d, in_=sr), inc=f"fo{par}", dn=16, waits=[ev_onT_tile])

                onT_free[par] = ev_fo

                if i == NT // 2 - 1:
                    ev_cc1 = pl.add("gpsimd", lambda g: g.collective_compute(
                        "AllGather", ALU.bypass, GROUPS,
                        [onT_snd[0].opt()], [onT_g[0].opt()]),
                        inc="cc1", dn=1,
                        waits=[("fo0", pl.counts["fo0"]), ("fo1", pl.counts["fo1"])])
                if i == NT - 1:
                    ev_cc2 = pl.add("gpsimd", lambda g: g.collective_compute(
                        "AllGather", ALU.bypass, GROUPS,
                        [onT_snd[1].opt()], [onT_g[1].opt()]),
                        inc="cc2", dn=1,
                        waits=[("fo0", pl.counts["fo0"]), ("fo1", pl.counts["fo1"])])

                if i < 4:
                    for c in range(i * 16, i * 16 + 16):
                        emit_scale_chunk(c)
                elif i == 4:
                    ev_ra1 = pl.add("scalar", lambda s: s.activation(
                        tmpb[:, 0:32], accr[:, 0:32], AF.Identity,
                        accum_out=accr[:, 64:65]), inc="a",
                        waits=[tern_state["stf"][0], tern_state["stf"][1]])
                    ev_ra2 = pl.add("scalar", lambda s: s.activation(
                        tmpb[:, 0:32], accr[:, 32:64], AF.Identity,
                        accum_out=accr[:, 65:66]), inc="a")
                    pl.add("tensor", lambda t: t.matmul(
                        psA[0][0:1, 0:1], accr[:, 64:65], ones_cf[:, :],
                        start=True, stop=False),
                        waits=[ev_ra1, ev_ra2, psA_ev[0]])
                    ev_tot = pl.add("tensor", lambda t: t.matmul(
                        psA[0][0:1, 2:3], accr[:, 65:66], ones_cf[:, :],
                        start=False, stop=True), inc="t")
                    pl.add("vector", lambda v: v.tensor_copy(
                        totc[:, 0:1], psA[0][0:1, 0:1]), inc="v", waits=[ev_tot])
                    ev_tc = pl.add("vector", lambda v: v.tensor_copy(
                        totc[:, 1:2], psA[0][0:1, 2:3]), inc="v")
                    psA_ev[0] = ev_tc
                    ev_sc_a = pl.add("scalar", lambda s: s.mul(
                        totc[:, 2:4], totc[:, 0:2], INV_NW), inc="a", waits=[ev_tc])
                    ev_sc12 = pl.add("vector", lambda v: v.tensor_tensor(
                        sc12[:, 0:2], totc[:, 2:4], eps2[:, :], ALU.max), inc="v",
                        waits=[ev_sc_a])
                    ev_t4 = None
                    for k, (col, mulv) in enumerate(((0, 0.5), (1, -0.5), (2, 0.5), (3, -0.5))):
                        ev_t4 = pl.add("scalar", lambda s, c_=col, m=mulv: s.mul(
                            thr4[:, c_:c_ + 1], sc12[:, c_ // 2:c_ // 2 + 1], m),
                            inc=("a" if k == 3 else None),
                            waits=([ev_sc12] if k == 0 else []))
                    ev_thr = pl.add("tensor", lambda t: t.matmul(
                        psA[1][:, 0:4], ones_rf[:, :], thr4[:, :],
                        start=True, stop=True), inc="t", waits=[ev_t4, psA_ev[1]])
                    ev_thrb = pl.add("scalar", lambda s: s.copy(
                        thrb[:, :], psA[1][:, 0:4]), inc="a", waits=[ev_thr])
                    psA_ev[1] = ev_thrb
                elif i < 13:
                    for ct in range((i - 5) * 4, (i - 5) * 4 + 4):
                        emit_w1_chunk(ct, ev_thrb)

            finalsA = dict(pl.counts)

            with nc.Block() as blkA:
                @blkA.sync
                def _(engx):
                    pl.realize("sync", engx)

                @blkA.gpsimd
                def _(engx):
                    pl.realize("gpsimd", engx)
                    for s in ("fx0", "fx1", "fo0", "fo1", "t", "v", "a"):
                        engx.wait_ge(sems[s], finalsA[s])
                    engx.sem_inc(sems["bar"], 1)

                @blkA.vector
                def _(engx):
                    pl.realize("vector", engx)

                @blkA.scalar
                def _(engx):
                    pl.realize("scalar", engx)

                @blkA.tensor
                def _(engx):
                    pl.realize("tensor", engx)

        with contextlib.ExitStack() as bctx:
            e = bctx.enter_context
            wo = e(nc.sbuf_tensor("wo", [128, H * DM], BF16))
            w2t = e(nc.sbuf_tensor("w2t", [128, FCH * DM], BF16))
            b1s = e(nc.sbuf_tensor("b1s", [128, FCH], F32))
            b2bf = e(nc.sbuf_tensor("b2bf", [1, DM], BF16))
            xnb = [e(nc.sbuf_tensor(f"xnb{i}", [128, KC * TB], BF16)) for i in range(2)]
            onb = [e(nc.sbuf_tensor(f"onb{i}", [128, H * TB], BF16)) for i in range(2)]
            xrs = [e(nc.sbuf_tensor(f"xrs{i}", [128, DM], F32)) for i in range(2)]
            outs = [e(nc.sbuf_tensor(f"outs{i}", [128, DM], F32)) for i in range(2)]
            hbuf = [e(nc.sbuf_tensor(f"hbuf{i}", [128, TB], BF16)) for i in range(3)]
            gsb = e(nc.sbuf_tensor("gsb", [128, 2 * DM], F32))
            sgt = e(nc.sbuf_tensor("sgt", [128, DM], F32))
            psH = [e(nc.psum_tensor(f"psH{i}", [128, TB], F32)) for i in range(2)]
            psO2 = [e(nc.psum_tensor(f"psO2{i}", [128, DM], F32)) for i in range(2)]
            psS = e(nc.psum_tensor("psS", [128, DM], F32))

            pl2 = Plan(sems, finalsA)
            BAR = ("bar", 1)
            pl2.counts["bar"] = 1

            pl2.add("gpsimd", lambda g: g.dma_start(
                out=wo[:, :].rearrange("p (he dm) -> p he dm", he=H),
                in_=woT.rearrange("(he p) dm -> p he dm", p=128)),
                inc="wB", dn=16, waits=[BAR])
            pl2.add("sync", lambda s: s.dma_start(out=b1s[:, :], in_=b1c),
                    inc="wBs", dn=16, waits=[BAR])
            pl2.add("gpsimd", lambda g: g.dma_start(out=b2bf[:, :], in_=b2r),
                    inc="wB", dn=16, waits=[BAR])
            EV_WB = [("wB", pl2.counts["wB"]), ("wBs", pl2.counts["wBs"])]
            ev_wsc = pl2.add("gpsimd", lambda g: g.tensor_tensor(
                wo[:, :], wo[:, :],
                nwb[:, 0:1].broadcast_to([128, H * DM]), ALU.mult),
                inc="g2", waits=[("wB", pl2.counts["wB"])])

            st_free = [None, None]
            w2rdy = [None] * FCH
            for c in range(FCH):
                srcp = w2T[c * 128:(c + 1) * 128, :]
                sl = c % 2
                ev_ld = pl2.add("sync", lambda s, sr=srcp, sl_=sl: s.dma_start(
                    out=stage[sl_][:, :], in_=sr), inc=f"su{sl}", dn=16,
                    waits=[st_free[sl], BAR])
                pl2.add("vector", lambda v, sl_=sl: v.tensor_single_scalar(
                    tmpb[:, :], stage[sl_][:, :], thrb[:, 3:4], ALU.is_lt),
                    inc="v2", waits=[ev_ld, BAR])
                ev_w = pl2.add("vector", lambda v, c_=c, sl_=sl: v.scalar_tensor_tensor(
                    w2t[:, c_ * DM:(c_ + 1) * DM], stage[sl_][:, :],
                    thrb[:, 2:3], tmpb[:, :], ALU.is_gt, ALU.subtract),
                    inc="v2")
                st_free[sl] = ev_w
                w2rdy[c] = ev_w

            xnb_free = [None, None]
            onb_free = [None, None]
            regs2 = {}

            def _mk_load2(eng):
                regs2[0] = nc.values_load(hsl[0:1, 0:1],
                                          engines=[mybir.EngineType.SP],
                                          skip_runtime_bounds_check=True)
                regs2[1] = nc.values_load(hsl[0:1, 1:2],
                                          engines=[mybir.EngineType.SP],
                                          skip_runtime_bounds_check=True)
                return eng.nop()
            pl2.add("sync", _mk_load2, waits=[BAR])

            def onb_fn(dst, s0, s1, sem):
                def f(eng):
                    with eng.If(regs2[0]):
                        eng.wait_ge(sems["cc1"], 1)
                        eng.dma_start(out=dst, in_=s0).then_inc(sems[sem], 16)
                    with eng.Else():
                        eng.wait_ge(sems["cc2"], 1)
                        eng.dma_start(out=dst, in_=s1).then_inc(sems[sem], 16)
                    return eng.nop()
                return f
            xr_free = [None, None]
            out_free = [None, None]
            psH_ev = [None, None]
            hb_free = [None, None, None]
            psS_ev = None
            psO2_ev = [None, None]
            gsb_last = [None, None]
            sgt_ev = None

            for i in range(NTB):
                t0 = i * TB
                par = i % 2
                src = xnT_d[:, t0:t0 + TB].rearrange("(c p) t -> p c t", p=128)
                dap = xnb[par][:, :].rearrange("p (c t) -> p c t", c=KC)
                ev_xnb = pl2.add("sync", lambda s, d=dap, sr=src: s.dma_start(
                    out=d, in_=sr), inc=f"bx{par}", dn=16,
                    waits=[xnb_free[par], None if i > 1 else ("fx0", finalsA["fx0"]),
                           None if i > 1 else ("fx1", finalsA["fx1"])])
                s0 = onT_g[0][:, t0:t0 + TB].rearrange("(h p) t -> p h t", p=128)
                s1 = onT_g[1][:, t0:t0 + TB].rearrange("(h p) t -> p h t", p=128)
                dap = onb[par][:, :].rearrange("p (h t) -> p h t", h=H)
                pl2.counts[f"bo{par}"] += 16
                ev_onb = (f"bo{par}", pl2.counts[f"bo{par}"])
                pl2.add("sync", onb_fn(dap, s0, s1, f"bo{par}"),
                        waits=[onb_free[par]])
                ev_xr = [None, None]
                for j2 in range(2):
                    src = xh_d[t0 + j2 * 128:t0 + j2 * 128 + 128, :]
                    ev_xr[j2] = pl2.add("gpsimd", lambda g, d=xrs[j2], sr=src: g.dma_start(
                        out=d[:, :], in_=sr), inc=f"br{j2}", dn=16,
                        waits=[xr_free[j2], BAR if i == 0 else None])

                ev_wo0 = None
                for he in range(H):
                    for n in range(2):
                        ev_wo0 = pl2.add("tensor", lambda t, he_=he, n_=n, p_=par: t.matmul(
                            psS[:, n_ * 512:(n_ + 1) * 512],
                            onb[p_][:, he_ * TB: he_ * TB + 128],
                            wo[:, he_ * DM + n_ * 512: he_ * DM + n_ * 512 + 512],
                            start=(he_ == 0), stop=(he_ == H - 1)),
                            inc=("t2" if (he == H - 1 and n == 1) else None),
                            waits=([ev_onb, psS_ev, ev_wsc] if he == 0 and n == 0 else []))

                gel_evs = {}

                def emit_ffn2(c2):
                    for j2 in range(2):
                        wts = [gel_evs[c2]]
                        if c2 == 0:
                            wts.append(psO2_ev[j2])
                            psO2_ev[j2] = None
                        if i == 0:
                            wts.append(w2rdy[c2])
                        for n in range(2):
                            pl2.add("tensor", lambda t, c_=c2, j_=j2, n_=n, sl_=c2 % 3: t.matmul(
                                psO2[j_][:, n_ * 512:(n_ + 1) * 512],
                                hbuf[sl_][:, j_ * 128:(j_ + 1) * 128],
                                w2t[:, c_ * DM + n_ * 512: c_ * DM + n_ * 512 + 512],
                                start=(c_ == 0), stop=False),
                                inc=("t2" if (j2 == 1 and n == 1) else None),
                                waits=(wts if n == 0 else []))

                for c in range(FCH):
                    sl = c % 3
                    ev_mm = None
                    for kc in range(KC):
                        ev_mm = pl2.add("tensor", lambda t, c_=c, kc_=kc, sl_=sl, p_=par: t.matmul(
                            psH[c_ % 2][:, :], w1t[:, kc_ * FF + c_ * 128: kc_ * FF + c_ * 128 + 128],
                            xnb[p_][:, kc_ * TB:(kc_ + 1) * TB],
                            start=(kc_ == 0), stop=(kc_ == KC - 1)),
                            inc=("t2" if kc == KC - 1 else None),
                            waits=([ev_xnb, psH_ev[c % 2], EV_WB] if kc == 0 else []))
                    if c == FCH - 1:
                        ev_ffn1_last = ev_mm
                    ev_gel = pl2.add("scalar", lambda s, c_=c, sl_=sl: s.activation(
                        hbuf[sl_][:, :], psH[c_ % 2][:, :], AF.Gelu, bias=b1s[:, c_:c_ + 1]),
                        inc="a2", waits=[ev_mm, hb_free[sl]])
                    gel_evs[c] = ev_gel
                    psH_ev[c % 2] = ev_gel
                    if c >= 2:
                        emit_ffn2(c - 2)
                        hb_free[(c - 2) % 3] = ("t2", pl2.counts["t2"])
                for c2 in (FCH - 2, FCH - 1):
                    emit_ffn2(c2)
                    hb_free[c2 % 3] = ("t2", pl2.counts["t2"])
                ev_ffn2 = None
                for j2 in range(2):
                    for n in range(2):
                        ev_ffn2 = pl2.add("tensor", lambda t, j_=j2, n_=n: t.matmul(
                            psO2[j_][:, n_ * 512:(n_ + 1) * 512], ones_rb[:, :],
                            b2bf[:, n_ * 512:n_ * 512 + 512], start=False, stop=True),
                            inc=("t2" if (j2 == 1 and n == 1) else None))
                ev_wo = [ev_wo0, None]
                for j2 in range(2):
                    ev_si = pl2.add("scalar", lambda s, j_=j2: s.activation(
                        gsb[:, j_ * DM:(j_ + 1) * DM], psO2[j_][:, :], AF.Silu),
                        inc="a2", waits=[ev_ffn2, gsb_last[j2]])
                    psO2_ev[j2] = ev_si
                    ev_sg = pl2.add("vector", lambda v, j_=j2: v.tensor_tensor(
                        sgt[:, :], psS[:, :], gsb[:, j_ * DM:(j_ + 1) * DM],
                        ALU.mult), inc="v2", waits=[ev_wo[j2], ev_si, sgt_ev])
                    gsb_last[j2] = ev_sg
                    if j2 == 0:
                        ev_w1 = None
                        for he in range(H):
                            for n in range(2):
                                ev_w1 = pl2.add("tensor", lambda t, he_=he, n_=n, p_=par: t.matmul(
                                    psS[:, n_ * 512:(n_ + 1) * 512],
                                    onb[p_][:, he_ * TB + 128: he_ * TB + 256],
                                    wo[:, he_ * DM + n_ * 512: he_ * DM + n_ * 512 + 512],
                                    start=(he_ == 0), stop=(he_ == H - 1)),
                                    inc=("t2" if (he == H - 1 and n == 1) else None),
                                    waits=([ev_sg] if he == 0 and n == 0 else []))
                        ev_wo[1] = ev_w1
                    else:
                        psS_ev = ev_sg
                    ev_add = pl2.add("gpsimd", lambda g, j_=j2: g.tensor_add(
                        outs[j_][:, :], sgt[:, :], xrs[j_][:, :]), inc="g2",
                        waits=[ev_sg, ev_xr[j2], out_free[j2]])
                    sgt_ev = ev_add
                    xr_free[j2] = ev_add
                    dstap = out_d[t0 + j2 * 128:t0 + j2 * 128 + 128, :]
                    ev_st = pl2.add("scalar", lambda s, d=dstap, sr=outs[j2]: s.dma_start(
                        out=d, in_=sr[:, :]), inc=f"bs{j2}", dn=16, waits=[ev_add])
                    out_free[j2] = ev_st
                xnb_free[par] = ev_ffn1_last
                onb_free[par] = ev_wo

            finalsB = dict(pl2.counts)

            with nc.Block() as blkB:
                @blkB.sync
                def _(engx):
                    pl2.realize("sync", engx)

                @blkB.gpsimd
                def _(engx):
                    pl2.realize("gpsimd", engx)
                    engx.wait_ge(sems["bs0"], finalsB["bs0"])
                    engx.wait_ge(sems["bs1"], finalsB["bs1"])

                @blkB.vector
                def _(engx):
                    pl2.realize("vector", engx)

                @blkB.scalar
                def _(engx):
                    pl2.realize("scalar", engx)

                @blkB.tensor
                def _(engx):
                    pl2.realize("tensor", engx)

    return nc


_NC_CACHE = None


def _get_nc():
    global _NC_CACHE
    if _NC_CACHE is None:
        _NC_CACHE = build()
    return _NC_CACHE


def kernel(L0_attn_hidden_states, L0_attn_Wq, L0_attn_Wk, L0_attn_Wv, L0_attn_Wo,
           L0_attn_norm_w, L0_ffn_W1, L0_ffn_b1, L0_ffn_W2, L0_ffn_b2):
    x = np.ascontiguousarray(np.asarray(L0_attn_hidden_states, dtype=np.float32))
    wq = np.ascontiguousarray(np.asarray(L0_attn_Wq, dtype=np.float32).T)
    wk = np.ascontiguousarray(np.asarray(L0_attn_Wk, dtype=np.float32).T)
    wv = np.ascontiguousarray(np.asarray(L0_attn_Wv, dtype=np.float32).T)
    wo = np.ascontiguousarray(np.asarray(L0_attn_Wo, dtype=np.float32).T)
    w1 = np.ascontiguousarray(np.asarray(L0_ffn_W1, dtype=np.float32).T)
    w2 = np.ascontiguousarray(np.asarray(L0_ffn_W2, dtype=np.float32).T)
    b1 = np.ascontiguousarray(np.asarray(L0_ffn_b1, dtype=np.float32).reshape(FCH, 128).T)
    b2 = np.ascontiguousarray(np.asarray(L0_ffn_b2, dtype=np.float32).reshape(1, DM))
    nw = np.ascontiguousarray(
        np.asarray(L0_attn_norm_w, dtype=np.float32).reshape(128, 1))

    nc = _get_nc()
    in_maps = []
    for c in range(8):
        b = c // 2
        half = c % 2
        es = slice(half * (HL * DH), (half + 1) * (HL * DH))
        in_maps.append({
            "x": np.ascontiguousarray(x[b]),
            "wqT": np.ascontiguousarray(wq[:, es]),
            "wkT": np.ascontiguousarray(wk[:, es]),
            "wvT": np.ascontiguousarray(wv[:, es]),
            "woT": wo, "w1T": w1, "w2T": w2, "b1c": b1, "b2r": b2, "nwb": nw,
            "xh": np.ascontiguousarray(x[b, half * (S // 2):(half + 1) * (S // 2)]),
            "hsl": np.array([[1 - half, half]], dtype=np.int32),
        })
    res = run_bass_kernel_spmd(nc, in_maps, core_ids=list(range(8)))
    out = np.empty((B, S, DM), dtype=np.float32)
    for b in range(B):
        out[b, 0:S // 2] = res.results[2 * b]["out"]
        out[b, S // 2:] = res.results[2 * b + 1]["out"]
    return out


# version 40 (final)
# speedup vs baseline: 1.1005x; 1.0306x over previous
import contextlib
import numpy as np
import ml_dtypes
import concourse.bass as bass
import concourse.mybir as mybir
from concourse.bass_utils import run_bass_kernel_spmd

F32 = mybir.dt.float32
BF16 = mybir.dt.bfloat16
AF = mybir.ActivationFunctionType
ALU = mybir.AluOpType
AX = mybir.AxisListType

B, S, DM, H, DH, FF = 4, 8192, 1024, 8, 128, 4096
KC = DM // 128
TILE = 512
NT = S // TILE
NB = TILE // 128
SW = 136
TB = 256
NTB = (S // 2) // TB
FCH = FF // 128
HL = H // 2
GROUPS = [[0, 1], [2, 3], [4, 5], [6, 7]]
LN_EPS = 1e-5
RMS_EPS = 1e-5
INV_NW = 1.0 / (FF * DM)


class Plan:

    ENGS = ("sync", "gpsimd", "vector", "scalar", "tensor")

    def __init__(self, sems, counts=None):
        self.sems = sems
        self.counts = dict(counts) if counts else {k: 0 for k in sems}
        self.ops = {e: [] for e in self.ENGS}

    def add(self, eng, fn, inc=None, dn=1, waits=()):
        ws = []
        for w in waits:
            if w is None:
                continue
            if isinstance(w, list):
                ws.extend(x for x in w if x is not None)
            else:
                ws.append(w)
        for (s, v) in ws:
            assert v <= self.counts[s], f"wait on future {s}={v}>{self.counts[s]}"
        ev = None
        if inc is not None:
            self.counts[inc] += dn
            ev = (inc, self.counts[inc])
        self.ops[eng].append((fn, inc, dn, tuple(ws)))
        return ev

    def realize(self, eng_name, eng):
        seen = {k: 0 for k in self.sems}
        for fn, inc, dn, ws in self.ops[eng_name]:
            for (s, v) in ws:
                if v > seen[s]:
                    eng.wait_ge(self.sems[s], v)
                    seen[s] = v
            inst = fn(eng)
            if inc is not None:
                inst.then_inc(self.sems[inc], dn)


def build():
    nc = bass.Bass("TRN2", target_bir_lowering=False, detect_race_conditions=False)
    x_d = nc.dram_tensor("x", [S, DM], F32, kind="ExternalInput").ap()
    wqT = nc.dram_tensor("wqT", [DM, HL * DH], F32, kind="ExternalInput").ap()
    wkT = nc.dram_tensor("wkT", [DM, HL * DH], F32, kind="ExternalInput").ap()
    wvT = nc.dram_tensor("wvT", [DM, HL * DH], F32, kind="ExternalInput").ap()
    woT = nc.dram_tensor("woT", [DM, DM], F32, kind="ExternalInput").ap()
    w1T = nc.dram_tensor("w1T", [DM, FF], F32, kind="ExternalInput").ap()
    w2T = nc.dram_tensor("w2T", [FF, DM], F32, kind="ExternalInput").ap()
    b1c = nc.dram_tensor("b1c", [128, FCH], F32, kind="ExternalInput").ap()
    b2r = nc.dram_tensor("b2r", [1, DM], F32, kind="ExternalInput").ap()
    nwb_d = nc.dram_tensor("nwb", [128, 1], F32, kind="ExternalInput").ap()
    xh_d = nc.dram_tensor("xh", [S // 2, DM], F32, kind="ExternalInput").ap()
    hsl_d = nc.dram_tensor("hsl", [1, 2], mybir.dt.int32, kind="ExternalInput").ap()
    out_d = nc.dram_tensor("out", [S // 2, DM], F32, kind="ExternalOutput").ap()

    id_np = np.eye(128, dtype=ml_dtypes.bfloat16)
    maskT_np = np.tile(np.triu(np.ones((128, 128), dtype=np.float32)), (1, 4)).astype(ml_dtypes.bfloat16)
    idb_d = nc.inline_tensor(id_np, "idb_c").ap()
    maskT_d = nc.inline_tensor(maskT_np, "maskT_c").ap()
    ones_cb_d = nc.inline_tensor(np.ones((128, 2), dtype=ml_dtypes.bfloat16), "ones_cb_c").ap()
    ones_rb_d = nc.inline_tensor(np.ones((1, 128), dtype=ml_dtypes.bfloat16), "ones_rb_c").ap()
    ones_cf_d = nc.inline_tensor(np.ones((128, 1), dtype=np.float32), "ones_cf_c").ap()
    ones_rf_d = nc.inline_tensor(np.ones((1, 128), dtype=np.float32), "ones_rf_c").ap()

    onT_snd = [nc.dram_tensor(f"onT_snd{r}", [HL * DH, S // 2], BF16, kind="Internal").ap()
               for r in range(2)]
    onT_g = [nc.dram_tensor(f"onT_g{r}", [DM, S // 2], BF16, kind="Internal").ap()
             for r in range(2)]
    xnT_d = nc.dram_tensor("xnT_s", [DM, S // 2], BF16, kind="Internal").ap()

    sem_names = ["c0", "cg", "x0", "x1", "fx0", "fx1", "fo0", "fo1",
                 "t", "v", "a", "g", "bar", "wB", "wBs", "bx0", "bx1", "bo0", "bo1",
                 "br0", "br1", "bs0", "bs1", "st0", "st1", "t2", "v2", "a2", "g2",
                 "cc1", "cc2", "su0", "su1"]

    with contextlib.ExitStack() as octx:
        oe = octx.enter_context
        sems = {n: oe(nc.semaphore(n)) for n in sem_names}

        idb = oe(nc.sbuf_tensor("idb", [128, 128], BF16))
        maskT = oe(nc.sbuf_tensor("maskT", [128, 512], BF16))
        ones_cb = oe(nc.sbuf_tensor("ones_cb", [128, 2], BF16))
        ones_rb = oe(nc.sbuf_tensor("ones_rb", [1, 128], BF16))
        ones_cf = oe(nc.sbuf_tensor("ones_cf", [128, 1], F32))
        ones_rf = oe(nc.sbuf_tensor("ones_rf", [1, 128], F32))
        nwb = oe(nc.sbuf_tensor("nwb_sb", [128, 1], F32))
        hsl = oe(nc.sbuf_tensor("hsl_sb", [1, 2], mybir.dt.int32))
        w1t = oe(nc.sbuf_tensor("w1t", [128, KC * FF], BF16))
        stage = [oe(nc.sbuf_tensor(f"stg{i}", [128, 1024], F32)) for i in range(2)]
        tmpb = oe(nc.sbuf_tensor("tmpb", [128, 1024], BF16))
        accr = oe(nc.sbuf_tensor("accr", [128, 68], F32))
        thr4 = oe(nc.sbuf_tensor("thr4", [1, 4], F32))
        sc12 = oe(nc.sbuf_tensor("sc12", [1, 2], F32))
        totc = oe(nc.sbuf_tensor("totc", [1, 4], F32))
        eps2 = oe(nc.sbuf_tensor("eps2", [1, 2], F32))
        thrb = oe(nc.sbuf_tensor("thrb", [128, 4], F32))

        with contextlib.ExitStack() as actx:
            e = actx.enter_context
            wq = e(nc.sbuf_tensor("wq", [128, KC * HL * DH], BF16))
            wk = e(nc.sbuf_tensor("wk", [128, KC * HL * DH], BF16))
            wv = e(nc.sbuf_tensor("wv", [128, KC * HL * DH], BF16))
            Sf = e(nc.sbuf_tensor("Sf", [128, HL * SW], F32))
            Sbf = [e(nc.sbuf_tensor(f"Sbf{i}", [128, HL * SW], BF16)) for i in range(2)]
            dbf_s = e(nc.sbuf_tensor("dbf_s", [128, HL * 128], BF16))
            dbf_z = e(nc.sbuf_tensor("dbf_z", [128, HL * 2], BF16))
            xt = [e(nc.sbuf_tensor(f"xt{i}", [128, NB * DM], F32)) for i in range(2)]
            xnf = [e(nc.sbuf_tensor(f"xnf{i}", [128, KC * TILE], BF16)) for i in range(2)]
            qf = e(nc.sbuf_tensor("qf", [128, HL * TILE], BF16))
            kf = e(nc.sbuf_tensor("kf", [128, HL * TILE], BF16))
            vt = e(nc.sbuf_tensor("vt", [128, NB * HL * DH], BF16))
            ktb = e(nc.sbuf_tensor("ktb", [128, NB * HL * 128], BF16))
            onsb = e(nc.sbuf_tensor("onsb", [128, NB * HL * 128], BF16))
            osb = e(nc.sbuf_tensor("osb", [128, NB * 512], F32))
            onT = [e(nc.sbuf_tensor(f"onT{i}", [128, HL * TILE], BF16)) for i in range(2)]
            xn_tm = [e(nc.sbuf_tensor(f"xn_tm{i}", [128, DM], BF16)) for i in range(2)]
            sq_scr = e(nc.sbuf_tensor("sq_scr", [128, DM], BF16))
            id_scr = e(nc.sbuf_tensor("id_scr", [128, DM], BF16))
            sc_sb = e(nc.sbuf_tensor("sc_sb", [128, 512], BF16))
            stats = e(nc.sbuf_tensor("stats", [128, 64], F32))
            atts = e(nc.sbuf_tensor("atts", [128, 320], F32))
            ones16 = e(nc.sbuf_tensor("ones16", [128, 32], F32))
            epsc = e(nc.sbuf_tensor("epsc", [128, 1], F32))
            psA = [e(nc.psum_tensor(f"psA{i}", [128, 512], F32)) for i in range(2)]
            psO = [e(nc.psum_tensor(f"psO{i}", [128, 512], F32)) for i in range(2)]
            psDen = e(nc.psum_tensor("psDen", [128, 16], F32))
            psSd = e(nc.psum_tensor("psSd", [128, 512], F32))
            psT2 = [e(nc.psum_tensor(f"psT{i}", [128, 512], BF16)) for i in range(2)]

            pl = Plan(sems)

            for dst, src in ((idb, idb_d), (maskT, maskT_d), (ones_cb, ones_cb_d),
                             (ones_rb, ones_rb_d), (ones_cf, ones_cf_d),
                             (ones_rf, ones_rf_d), (nwb, nwb_d), (hsl, hsl_d)):
                pl.add("sync", lambda s, d=dst, sr=src: s.dma_start(out=d[:, :], in_=sr),
                       inc="c0", dn=16)
            for dst, src in ((wq, wqT), (wk, wkT), (wv, wvT)):
                ap = src.rearrange("(kc p) e -> p kc e", p=128)
                dap = dst[:, :].rearrange("p (kc e) -> p kc e", kc=KC)

                pl.add("gpsimd", lambda g, d=dap, sr=ap: g.dma_start(out=d, in_=sr),
                       inc="cg", dn=16)
            EV_C0 = [("c0", pl.counts["c0"]), ("cg", pl.counts["cg"])]

            pl.add("vector", lambda v: v.memset(ones16[:, :], 1.0), inc="v")
            pl.add("vector", lambda v: v.memset(epsc[:, :], LN_EPS), inc="v")
            regs = {}

            def _mk_load(engname, enum):
                def f(eng):
                    regs[(engname, 0)] = nc.values_load(
                        hsl[0:1, 0:1], engines=[enum], skip_runtime_bounds_check=True)
                    regs[(engname, 1)] = nc.values_load(
                        hsl[0:1, 1:2], engines=[enum], skip_runtime_bounds_check=True)
                    return eng.nop()
                return f
            pl.add("scalar", _mk_load("scalar", mybir.EngineType.Activation),
                   waits=[("c0", pl.counts["c0"])])
            pl.add("sync", _mk_load("sync", mybir.EngineType.SP),
                   waits=[("c0", pl.counts["c0"])])

            def spill_fn(engname, dst, srcap, want, sem):
                def f(eng):
                    with eng.If(regs[(engname, want)]):
                        eng.dma_start(out=dst, in_=srcap).then_inc(sems[sem], 16)
                    with eng.Else():
                        eng.sem_inc(sems[sem], 16)
                    return eng.nop()
                return f
            pl.add("vector", lambda v: v.memset(atts[:, :], 0.0), inc="v")
            pl.add("vector", lambda v: v.memset(eps2[:, :], 1e-8), inc="v")
            pl.add("vector", lambda v: v.memset(Sf[:, :], 0.0), inc="v")
            pl.add("vector", lambda v: v.memset(Sbf[0][:, :], 0.0), inc="v")
            pl.add("vector", lambda v: v.memset(Sbf[1][:, :], 0.0), inc="v")
            pl.add("vector", lambda v: v.memset(dbf_s[:, :], 0.0), inc="v")
            ev_sb0 = pl.add("vector", lambda v: v.memset(dbf_z[:, :], 0.0), inc="v")

            tern_state = {"stf": [None, None], "ld": 0}

            def tern_stage_load(src_ap, sl):
                w = [tern_state["stf"][sl], EV_C0]
                return pl.add("scalar", lambda s, sr=src_ap, sl_=sl: s.dma_start(
                    out=stage[sl_][:, :], in_=sr), inc=f"st{sl}", dn=16, waits=w)

            def emit_scale_chunk(c):
                if c < 32:
                    srcp = w1T[(c % 8) * 128:(c % 8) * 128 + 128,
                               (c // 8) * 1024:(c // 8) * 1024 + 1024]
                else:
                    srcp = w2T[(c - 32) * 128:(c - 31) * 128, :]
                sl = c % 2
                ev_ld = tern_stage_load(srcp, sl)
                ev_r = pl.add("vector", lambda v, c_=c, sl_=sl: v.tensor_reduce(
                    accr[:, c_:c_ + 1], stage[sl_][:, :], AX.X, ALU.add,
                    apply_absolute_value=True), inc="v", waits=[ev_ld])
                tern_state["stf"][sl] = ev_r

            w1rdy = [None] * 4

            def emit_w1_chunk(ct, ev_thrb_):
                fi, kc = ct // 8, ct % 8
                srcp = w1T[kc * 128:(kc + 1) * 128, fi * 1024:(fi + 1) * 1024]
                sl = ct % 2
                ev_ld = tern_stage_load(srcp, sl)
                pl.add("vector", lambda v, sl_=sl: v.tensor_single_scalar(
                    tmpb[:, :], stage[sl_][:, :], thrb[:, 1:2], ALU.is_lt),
                    inc="v", waits=[ev_ld, ev_thrb_])
                ev_w = pl.add("vector", lambda v, kc_=kc, fi_=fi, sl_=sl: v.scalar_tensor_tensor(
                    w1t[:, kc_ * FF + fi_ * 1024: kc_ * FF + fi_ * 1024 + 1024],
                    stage[sl_][:, :], thrb[:, 0:1], tmpb[:, :],
                    ALU.is_gt, ALU.subtract), inc="v")
                tern_state["stf"][sl] = ev_w
                if kc == 7:
                    w1rdy[fi] = ev_w
                return ev_w

            xt_free = [None, None]
            xnf_free = [None, None]
            onT_free = [None, None]
            last_sbf = ev_sb0
            sbf_ev = [ev_sb0, ev_sb0]
            dbf_ev = ev_sb0
            psA_ev = [None, None]
            psO_ev = [None, None]
            psDen_ev = None
            psSd_ev = None
            psT_free = [None, None]
            last_stats_w = None
            last_var_g = None
            last_ms_ev = None
            last_dn_ev = None
            last_ep_act = None
            last_ab = None
            osb_war = None
            last_xntp = None
            blkctr = 0

            def emit_xload(ti):
                par2 = ti % 2
                evs = []
                for b in range(NB):
                    srcx = x_d[ti * TILE + b * 128: ti * TILE + (b + 1) * 128, :]
                    dapx = xt[par2][:, b * DM:(b + 1) * DM]
                    ev = pl.add("sync", lambda s, d=dapx, sr=srcx: s.dma_start(
                        out=d, in_=sr), inc=f"x{par2}", dn=16,
                        waits=[xt_free[par2]] if b == 0 else [])
                    evs.append(ev)
                xload_ev[par2] = evs

            def emit_ln_stats(ti, j):
                nonlocal last_stats_w, last_var_g
                par2 = ti % 2
                B0 = 32 * par2
                xblk = xt[par2][:, j * DM:(j + 1) * DM]
                pl.add("scalar", lambda s, xb=xblk, j_=j, b_=B0: s.activation(
                    sq_scr[:, :], xb, AF.Square, scale=1.0 / 32.0,
                    accum_out=stats[:, b_ + j_:b_ + j_ + 1]), inc="a",
                    waits=[xload_ev[par2][j]] + ([last_stats_w, last_var_g] if j == 0 else []))
                lnp[(ti, j)] = pl.add("scalar", lambda s, xb=xblk, j_=j, b_=B0: s.activation(
                    id_scr[:, :], xb, AF.Identity, scale=1.0 / DM,
                    accum_out=stats[:, b_ + 4 + j_:b_ + 5 + j_]), inc="a")

            def emit_ln_chain(ti):
                nonlocal last_var_g
                par2 = ti % 2
                B0 = 32 * par2
                ev_sqid = lnp.pop((ti, NB - 1))
                ev_m2 = pl.add("vector", lambda v, b_=B0: v.tensor_tensor(
                    stats[:, b_ + 8:b_ + 12], stats[:, b_ + 4:b_ + 8],
                    stats[:, b_ + 4:b_ + 8], ALU.mult),
                    inc="v", waits=[ev_sqid, last_var_g])
                ev_var = pl.add("gpsimd", lambda g, b_=B0: g.tensor_tensor(
                    stats[:, b_ + 12:b_ + 16], stats[:, b_:b_ + 4],
                    stats[:, b_ + 8:b_ + 12], ALU.subtract),
                    inc="g", waits=[ev_m2])
                last_var_g = ev_var
                ev_sdl = pl.add("scalar", lambda s, b_=B0: s.activation(
                    stats[:, b_ + 16:b_ + 20], stats[:, b_ + 12:b_ + 16], AF.Sqrt,
                    bias=epsc[:, 0:1]), inc="a", waits=[ev_var])
                ev_rr = pl.add("vector", lambda v, b_=B0: v.reciprocal(
                    stats[:, b_ + 20:b_ + 24], stats[:, b_ + 16:b_ + 20]),
                    inc="v", waits=[ev_sdl])
                ev_rs = pl.add("scalar", lambda s, b_=B0: s.copy(
                    stats[:, b_ + 24:b_ + 28], stats[:, b_ + 20:b_ + 24]),
                    inc="a", waits=[ev_rr])
                lnp[(ti, "rs")] = ev_rs

            def emit_ln_xn(ti, half):
                nonlocal last_stats_w, xnf_ready, xnf_free
                par2 = ti % 2
                B0 = 32 * par2
                ev_rs = lnp[(ti, "rs")] if half == 0 else lnp.pop((ti, "rs"))
                lc = (ti * TILE) % (S // 2)
                want = 0 if ti * TILE < S // 2 else 1
                last_xn = None
                ev_last_evac = None
                for j in (2 * half, 2 * half + 1):
                    xblk = xt[par2][:, j * DM:(j + 1) * DM]
                    ev_xn = pl.add("vector", lambda v, xb=xblk, j_=j, b_=B0, p_=par2: v.tensor_scalar(
                        xn_tm[p_][:, :], xb, stats[:, b_ + 4 + j_:b_ + 5 + j_],
                        stats[:, b_ + 24 + j_:b_ + 25 + j_],
                        ALU.subtract, ALU.mult), inc="v",
                        waits=[ev_rs, last_xn_tp[par2]])
                    for q in range(2):
                        sl = q
                        ev_tp = None
                        for c4 in range(4):
                            c = q * 4 + c4
                            ev_tp = pl.add("tensor", lambda t, c_=c, c4_=c4, sl_=sl, p_=par2: t.transpose(
                                psT2[sl_][:, c4_ * 128:(c4_ + 1) * 128],
                                xn_tm[p_][:, c_ * 128:(c_ + 1) * 128], idb[:, :]),
                                inc=("t" if c4 == 3 else None),
                                waits=([ev_xn, psT_free[sl]] if c4 == 0 else []))
                        qq = q
                        ev_ev = pl.add("scalar", lambda s, q_=qq, j_=j, sl_=sl, p_=par2: s.copy(
                            xnf[p_][:, :].rearrange("p (c t) -> p c t", c=KC)[:, q_ * 4:q_ * 4 + 4, j_ * 128:j_ * 128 + 128],
                            psT2[sl_][:, :].rearrange("p (c t) -> p c t", c=4)),
                            inc="a", waits=[ev_tp, xnf_free[par2] if j == 0 and q == 0 else None])
                        psT_free[sl] = ev_ev
                        ev_last_evac = ev_ev
                        last_xn_tp[par2] = ev_tp
                    last_xn = ev_xn
                if half == 0:
                    return
                xnf_free[par2] = None
                last_stats_w = ev_xn
                xt_free[par2] = [ev_xn, ev_rs]
                xnf_ready[par2] = ev_last_evac
                dstap = xnT_d[:, lc:lc + TILE].rearrange("(c p) t -> p c t", p=128)
                srcap = xnf[par2][:, :].rearrange("p (c t) -> p c t", c=KC)
                pl.counts[f"fx{par2}"] += 16
                ev_fx = (f"fx{par2}", pl.counts[f"fx{par2}"])
                pl.add("sync", spill_fn("sync", dstap, srcap, want, f"fx{par2}"),
                       waits=[ev_last_evac])
                xnf_free[par2] = ev_fx

            xload_ev = [None, None]
            xnf_ready = [None, None]
            last_xn_tp = [None, None]
            ring_unused = None
            lnp = {}
            emit_xload(0)
            for b in range(NB):
                emit_ln_stats(0, b)
            emit_ln_chain(0)
            emit_ln_xn(0, 0)
            emit_ln_xn(0, 1)

            for i in range(NT):
                t0 = i * TILE
                par = i % 2
                lc = t0 % (S // 2)
                want = 0 if t0 < S // 2 else 1
                if i + 1 < NT:
                    emit_xload(i + 1)
                ev_xnf_ready = xnf_ready[par]

                k_evac = [None] * HL
                v_evac = [None] * NB
                WH = HL * DH
                for wi, (wmat, dstf) in enumerate(((wq, qf), (wk, kf))):
                    for h in range(HL):
                        sl = h % 2
                        ev_mm = None
                        for kc in range(KC):
                            ev_mm = pl.add("tensor", lambda t, h_=h, kc_=kc, sl_=sl, wm=wmat, p_=par: t.matmul(
                                psA[sl_][:, 0:512],
                                wm[:, kc_ * WH + h_ * 128: kc_ * WH + h_ * 128 + 128],
                                xnf[p_][:, kc_ * TILE:(kc_ + 1) * TILE],
                                start=(kc_ == 0), stop=(kc_ == KC - 1)),
                                inc=("t" if kc == KC - 1 else None),
                                waits=([ev_xnf_ready, EV_C0, psA_ev[sl]] if kc == 0 else []))
                        ev_ev = pl.add("vector", lambda v, h_=h, sl_=sl, df=dstf: v.tensor_copy(
                            df[:, h_ * TILE:(h_ + 1) * TILE], psA[sl_][:, 0:512]),
                            inc="v", waits=[ev_mm])
                        psA_ev[sl] = ev_ev
                        if wi == 1:
                            k_evac[h] = ev_ev
                for j in range(NB):
                    sl = j % 2
                    ev_mm = None
                    for kc in range(KC):
                        ev_mm = pl.add("tensor", lambda t, j_=j, kc_=kc, sl_=sl, p_=par: t.matmul(
                            psA[sl_][:, 0:512],
                            xnf[p_][:, kc_ * TILE + j_ * 128: kc_ * TILE + j_ * 128 + 128],
                            wv[:, kc_ * WH: kc_ * WH + 512],
                            start=(kc_ == 0), stop=(kc_ == KC - 1)),
                            inc=("t" if kc == KC - 1 else None),
                            waits=([ev_xnf_ready, EV_C0, psA_ev[sl]] if kc == 0 else []))
                    ev_ev = pl.add("vector", lambda v, j_=j, sl_=sl: v.tensor_copy(
                        vt[:, j_ * 512: j_ * 512 + 512],
                        psA[sl_][:, 0:512]), inc="v", waits=[ev_mm])
                    psA_ev[sl] = ev_ev
                    v_evac[j] = ev_ev

                ktb_ev = [None] * NB
                for j in range(NB):
                    sl = j % 2
                    ev_ktp = None
                    for h in range(HL):
                        ksl = kf[:, h * TILE + j * 128: h * TILE + j * 128 + 128]
                        ev_ktp = pl.add("tensor", lambda t, ks=ksl, h_=h, sl_=sl: t.transpose(
                            psT2[sl_][:, h_ * 128:(h_ + 1) * 128], ks, idb[:, :]),
                            inc=("t" if h == HL - 1 else None),
                            waits=([k_evac[HL - 1], psT_free[sl]] if h == 0 else []))
                    ev_kte = pl.add("scalar", lambda s, j_=j, sl_=sl: s.copy(
                        ktb[:, j_ * 512:(j_ + 1) * 512],
                        psT2[sl_][:, :]), inc="a", waits=[ev_ktp])
                    psT_free[sl] = ev_kte
                    ktb_ev[j] = ev_kte

                for j in range(NB):
                    p = (blkctr >> 1) & 1
                    scb = 0
                    ev_sc = None
                    for h in range(HL):
                        qsl = qf[:, h * TILE + j * 128: h * TILE + j * 128 + 128]
                        ksl = kf[:, h * TILE + j * 128: h * TILE + j * 128 + 128]
                        ev_sc = pl.add("tensor", lambda t, ks=ksl, qs=qsl, h_=h, b_=scb: t.matmul(
                            psA[b_][:, h_ * 128:h_ * 128 + 128], ks, qs,
                            start=(h_ == 0), stop=(h_ == HL - 1)),
                            inc=("t" if h == HL - 1 else None),
                            waits=([psA_ev[scb]] if h == 0 else []))
                    psA_ev[scb] = None
                    ev_me = pl.add("vector", lambda v, b_=scb: v.tensor_tensor(
                        sc_sb[:, :], psA[b_][:, 0:512], maskT[:, :], ALU.mult),
                        inc="v", waits=[ev_sc])
                    psA_ev[scb] = ev_me
                    for h in range(HL):
                        qsl = qf[:, h * TILE + j * 128: h * TILE + j * 128 + 128]
                        wts = []
                        if h == 0:
                            wts = [sbf_ev[p], dbf_ev, psO_ev[p]]
                            psO_ev[p] = None
                        pl.add("tensor", lambda t, qs=qsl, h_=h, p_=p: t.matmul(
                            psO[p_][:, h_ * 128:h_ * 128 + 128], qs,
                            Sbf[p_][:, h_ * SW:h_ * SW + 128], start=(h_ == 0), stop=False),
                            waits=wts)
                        pl.add("tensor", lambda t, qs=qsl, h_=h, p_=p: t.matmul(
                            psO[p_][:, h_ * 128:h_ * 128 + 128], qs,
                            dbf_s[:, h_ * 128:h_ * 128 + 128], start=False, stop=False))
                    ev_o2 = None
                    for h in range(HL):
                        vsl = vt[:, j * 512 + h * 128: j * 512 + h * 128 + 128]
                        ev_o2 = pl.add("tensor", lambda t, h_=h, vs=vsl, p_=p: t.matmul(
                            psO[p_][:, h_ * 128:h_ * 128 + 128],
                            sc_sb[:, h_ * 128:h_ * 128 + 128], vs,
                            start=False, stop=(h_ == HL - 1)),
                            inc=("t" if h == HL - 1 else None),
                            waits=([ev_me, v_evac[j]] if h == 0 else []))
                    for h in range(HL):
                        vsl = vt[:, j * 512 + h * 128: j * 512 + h * 128 + 128]
                        pl.add("tensor", lambda t, j_=j, h_=h, vs=vsl: t.matmul(
                            psSd[:, h_ * 128:h_ * 128 + 128],
                            ktb[:, (j_ * HL + h_) * 128:(j_ * HL + h_) * 128 + 128], vs,
                            start=(h_ == 0), stop=(h_ == HL - 1)),
                            waits=([ktb_ev[j], psSd_ev] if h == 0 else []))
                    for h in range(HL):
                        qsl = qf[:, h * TILE + j * 128: h * TILE + j * 128 + 128]
                        pl.add("tensor", lambda t, qs=qsl, h_=h, p_=p: t.matmul(
                            psDen[:, 2 * h_:2 * h_ + 2], qs,
                            Sbf[p_][:, h_ * SW + 128:h_ * SW + 130],
                            start=(h_ == 0), stop=False),
                            waits=([psDen_ev] if h == 0 else []))
                        pl.add("tensor", lambda t, qs=qsl, h_=h: t.matmul(
                            psDen[:, 2 * h_:2 * h_ + 2], qs,
                            dbf_z[:, 2 * h_:2 * h_ + 2],
                            start=False, stop=False))
                    psDen_ev = None
                    for h in range(HL):
                        pl.add("tensor", lambda t, h_=h: t.matmul(
                            psDen[:, 2 * h_:2 * h_ + 1],
                            sc_sb[:, h_ * 128:h_ * 128 + 128], ones_cb[:, 0:1],
                            start=False, stop=False))
                    ev_z = None
                    for h in range(HL):
                        ev_z = pl.add("tensor", lambda t, j_=j, h_=h: t.matmul(
                            psDen[:, 8 + 2 * h_:10 + 2 * h_],
                            ktb[:, (j_ * HL + h_) * 128:(j_ * HL + h_) * 128 + 128],
                            ones_cb[:, 0:2], start=False, stop=(h_ == HL - 1)),
                            inc=("t" if h == HL - 1 else None))
                    ev_dbf1 = pl.add("vector", lambda v: v.tensor_copy(
                        dbf_s[:, :], psSd[:, 0:512]), inc="v", waits=[ev_z])
                    ev_dbf2 = pl.add("vector", lambda v: v.tensor_copy(
                        dbf_z[:, :], psDen[:, 8:16]), inc="v")
                    dbf_ev = ev_dbf2
                    sfv = Sf[:, 0:HL * SW].rearrange("p (h w) -> p h w", w=SW)
                    ev_sa1 = pl.add("vector", lambda v, s_=sfv: v.tensor_tensor(
                        s_[:, :, 0:128], s_[:, :, 0:128],
                        psSd[:, 0:512].rearrange("p (h d) -> p h d", d=128), ALU.add),
                        inc="v", waits=[ev_z, last_sbf])
                    ev_sa2 = pl.add("vector", lambda v, s_=sfv: v.tensor_tensor(
                        s_[:, :, 128:130], s_[:, :, 128:130],
                        psDen[:, 8:16].rearrange("p (h d) -> p h d", d=2), ALU.add),
                        inc="v")
                    psSd_ev = [ev_sa1, ev_dbf1]
                    ev_sbf = pl.add("gpsimd", lambda g, p_=p: g.tensor_copy(
                        Sbf[p_][:, 0:HL * SW], Sf[:, 0:HL * SW]), inc="g",
                        waits=[ev_sa2])
                    last_sbf = ev_sbf
                    sbf_ev[p] = ev_sbf
                    ev_oev = pl.add("scalar", lambda s, j_=j, p_=p: s.copy(
                        osb[:, j_ * 512:(j_ + 1) * 512], psO[p_][:, 0:512]),
                        inc="a", waits=[ev_o2, osb_war if j == 0 else None])
                    psO_ev[p] = ev_oev
                    for h in range(HL):
                        pl.add("scalar", lambda s, j_=j, h_=h: s.activation(
                            sq_scr[:, 0:128], osb[:, j_ * 512 + h_ * 128: j_ * 512 + h_ * 128 + 128],
                            AF.Square, scale=0.08838834764831845,
                            accum_out=atts[:, j_ * 8 + 2 * h_:j_ * 8 + 2 * h_ + 1]),
                            inc=None, waits=[last_ms_ev if j == 0 and h == 0 else None])
                    ev_ab = pl.add("scalar", lambda s, j_=j: s.activation(
                        atts[:, 64 + j_ * 8:72 + j_ * 8], psDen[:, 0:8], AF.Abs), inc="a",
                        waits=[last_dn_ev if j == 0 else None])
                    psDen_ev = [ev_ab, ev_sa2, ev_dbf2]
                    last_ab = ev_ab
                    blkctr += 2
                    if i + 1 < NT:
                        emit_ln_stats(i + 1, j)

                if i + 1 < NT:
                    emit_ln_chain(i + 1)
                    emit_ln_xn(i + 1, 0)
                    emit_ln_xn(i + 1, 1)

                ev_dn = pl.add("vector", lambda v: v.tensor_tensor(
                    atts[:, 128:160], atts[:, 64:96], ones16[:, :], ALU.max),
                    inc="v", waits=[last_ab, last_ep_act])
                last_dn_ev = ev_dn
                ev_ed = pl.add("scalar", lambda s: s.activation(
                    atts[:, 160:192], atts[:, 128:160], AF.Square,
                    scale=0.0031622776601683794), inc="a", waits=[ev_dn])
                ev_ms = pl.add("vector", lambda v: v.tensor_tensor(
                    atts[:, 192:224], atts[:, 160:192], atts[:, 0:32], ALU.add),
                    inc="v", waits=[ev_ed])
                last_ms_ev = ev_ms
                ev_sdm = pl.add("scalar", lambda s: s.activation(
                    atts[:, 224:256], atts[:, 192:224], AF.Sqrt), inc="a",
                    waits=[ev_ms])
                ev_ar = pl.add("vector", lambda v: v.reciprocal(
                    atts[:, 256:288], atts[:, 224:256]), inc="v", waits=[ev_sdm])
                ev_al = pl.add("scalar", lambda s: s.copy(
                    atts[:, 288:304].unsqueeze(-1),
                    atts[:, 256:288].rearrange("p (n two) -> p n two", two=2)[:, :, 0:1]),
                    inc="a", waits=[ev_ar])
                last_ep_act = ev_al
                ev_on_last = None
                on_evs = []
                for j in range(NB):
                    ev_on = None
                    for h in range(HL):
                        col = 288 + j * 4 + h
                        ev_on = pl.add("vector", lambda v, j_=j, h_=h, c_=col: v.tensor_scalar(
                            onsb[:, j_ * 512 + h_ * 128: j_ * 512 + h_ * 128 + 128],
                            osb[:, j_ * 512 + h_ * 128: j_ * 512 + h_ * 128 + 128],
                            atts[:, c_:c_ + 1], None, ALU.mult, ALU.bypass),
                            inc="v", waits=([ev_al] if j == 0 and h == 0 else []))
                    on_evs.append(ev_on)
                    ev_on_last = ev_on
                for j in range(NB):
                    sl = j % 2
                    ev_otp = None
                    for h in range(HL):
                        ev_otp = pl.add("tensor", lambda t, j_=j, h_=h, sl_=sl: t.transpose(
                            psT2[sl_][:, h_ * 128:(h_ + 1) * 128],
                            onsb[:, j_ * 512 + h_ * 128: j_ * 512 + h_ * 128 + 128],
                            idb[:, :]), inc=("t" if h == HL - 1 else None),
                            waits=([on_evs[j], psT_free[sl]] if h == 0 else []))
                    ev_oevT = pl.add("scalar", lambda s, sl_=sl, j_=j, p_=par: s.copy(
                        onT[p_][:, :].rearrange("p (h t) -> p h t", h=HL)[:, :, j_ * 128:j_ * 128 + 128],
                        psT2[sl_][:, :].rearrange("p (h t) -> p h t", h=HL)),
                        inc="a", waits=[ev_otp, onT_free[par] if j == 0 else None])
                    psT_free[sl] = ev_oevT
                onT_free[par] = None
                osb_war = ev_on_last
                ev_onT_tile = ev_oevT

                dstap = onT_snd[want][:, lc:lc + TILE].rearrange("(h p) t -> p h t", p=128)
                srcap = onT[par][:, :].rearrange("p (h t) -> p h t", h=HL)
                ev_fo = pl.add("scalar", lambda s, d=dstap, sr=srcap: s.dma_start(
                    out=d, in_=sr), inc=f"fo{par}", dn=16, waits=[ev_onT_tile])

                onT_free[par] = ev_fo

                if i == NT // 2 - 1:
                    ev_cc1 = pl.add("gpsimd", lambda g: g.collective_compute(
                        "AllGather", ALU.bypass, GROUPS,
                        [onT_snd[0].opt()], [onT_g[0].opt()]),
                        inc="cc1", dn=1,
                        waits=[("fo0", pl.counts["fo0"]), ("fo1", pl.counts["fo1"])])
                if i == NT - 1:
                    ev_cc2 = pl.add("gpsimd", lambda g: g.collective_compute(
                        "AllGather", ALU.bypass, GROUPS,
                        [onT_snd[1].opt()], [onT_g[1].opt()]),
                        inc="cc2", dn=1,
                        waits=[("fo0", pl.counts["fo0"]), ("fo1", pl.counts["fo1"])])

                if i < 4:
                    for c in range(i * 16, i * 16 + 16):
                        emit_scale_chunk(c)
                elif i == 4:
                    ev_ra1 = pl.add("scalar", lambda s: s.activation(
                        tmpb[:, 0:32], accr[:, 0:32], AF.Identity,
                        accum_out=accr[:, 64:65]), inc="a",
                        waits=[tern_state["stf"][0], tern_state["stf"][1]])
                    ev_ra2 = pl.add("scalar", lambda s: s.activation(
                        tmpb[:, 0:32], accr[:, 32:64], AF.Identity,
                        accum_out=accr[:, 65:66]), inc="a")
                    pl.add("tensor", lambda t: t.matmul(
                        psA[0][0:1, 0:1], accr[:, 64:65], ones_cf[:, :],
                        start=True, stop=False),
                        waits=[ev_ra1, ev_ra2, psA_ev[0]])
                    ev_tot = pl.add("tensor", lambda t: t.matmul(
                        psA[0][0:1, 2:3], accr[:, 65:66], ones_cf[:, :],
                        start=False, stop=True), inc="t")
                    pl.add("vector", lambda v: v.tensor_copy(
                        totc[:, 0:1], psA[0][0:1, 0:1]), inc="v", waits=[ev_tot])
                    ev_tc = pl.add("vector", lambda v: v.tensor_copy(
                        totc[:, 1:2], psA[0][0:1, 2:3]), inc="v")
                    psA_ev[0] = ev_tc
                    ev_sc_a = pl.add("scalar", lambda s: s.mul(
                        totc[:, 2:4], totc[:, 0:2], INV_NW), inc="a", waits=[ev_tc])
                    ev_sc12 = pl.add("vector", lambda v: v.tensor_tensor(
                        sc12[:, 0:2], totc[:, 2:4], eps2[:, :], ALU.max), inc="v",
                        waits=[ev_sc_a])
                    ev_t4 = None
                    for k, (col, mulv) in enumerate(((0, 0.5), (1, -0.5), (2, 0.5), (3, -0.5))):
                        ev_t4 = pl.add("scalar", lambda s, c_=col, m=mulv: s.mul(
                            thr4[:, c_:c_ + 1], sc12[:, c_ // 2:c_ // 2 + 1], m),
                            inc=("a" if k == 3 else None),
                            waits=([ev_sc12] if k == 0 else []))
                    ev_thr = pl.add("tensor", lambda t: t.matmul(
                        psA[1][:, 0:4], ones_rf[:, :], thr4[:, :],
                        start=True, stop=True), inc="t", waits=[ev_t4, psA_ev[1]])
                    ev_thrb = pl.add("scalar", lambda s: s.copy(
                        thrb[:, :], psA[1][:, 0:4]), inc="a", waits=[ev_thr])
                    psA_ev[1] = ev_thrb
                elif i < 13:
                    for ct in range((i - 5) * 4, (i - 5) * 4 + 4):
                        emit_w1_chunk(ct, ev_thrb)

            finalsA = dict(pl.counts)

            with nc.Block() as blkA:
                @blkA.sync
                def _(engx):
                    pl.realize("sync", engx)

                @blkA.gpsimd
                def _(engx):
                    pl.realize("gpsimd", engx)
                    for s in ("fx0", "fx1", "fo0", "fo1", "t", "v", "a"):
                        engx.wait_ge(sems[s], finalsA[s])
                    engx.sem_inc(sems["bar"], 1)

                @blkA.vector
                def _(engx):
                    pl.realize("vector", engx)

                @blkA.scalar
                def _(engx):
                    pl.realize("scalar", engx)

                @blkA.tensor
                def _(engx):
                    pl.realize("tensor", engx)

        with contextlib.ExitStack() as bctx:
            e = bctx.enter_context
            wo = e(nc.sbuf_tensor("wo", [128, H * DM], BF16))
            w2t = e(nc.sbuf_tensor("w2t", [128, FCH * DM], BF16))
            b1s = e(nc.sbuf_tensor("b1s", [128, FCH], F32))
            b2bf = e(nc.sbuf_tensor("b2bf", [1, DM], BF16))
            xnb = [e(nc.sbuf_tensor(f"xnb{i}", [128, KC * TB], BF16)) for i in range(2)]
            onb = [e(nc.sbuf_tensor(f"onb{i}", [128, H * TB], BF16)) for i in range(2)]
            xrs = [e(nc.sbuf_tensor(f"xrs{i}", [128, DM], F32)) for i in range(2)]
            outs = [e(nc.sbuf_tensor(f"outs{i}", [128, DM], F32)) for i in range(2)]
            hbuf = [e(nc.sbuf_tensor(f"hbuf{i}", [128, TB], BF16)) for i in range(3)]
            gsb = e(nc.sbuf_tensor("gsb", [128, 2 * DM], F32))
            sgt = e(nc.sbuf_tensor("sgt", [128, DM], F32))
            psH = [e(nc.psum_tensor(f"psH{i}", [128, TB], F32)) for i in range(2)]
            psO2 = [e(nc.psum_tensor(f"psO2{i}", [128, DM], F32)) for i in range(2)]
            psS = e(nc.psum_tensor("psS", [128, DM], F32))

            pl2 = Plan(sems, finalsA)
            BAR = ("bar", 1)
            pl2.counts["bar"] = 1

            pl2.add("gpsimd", lambda g: g.dma_start(
                out=wo[:, :].rearrange("p (he dm) -> p he dm", he=H),
                in_=woT.rearrange("(he p) dm -> p he dm", p=128)),
                inc="wB", dn=16, waits=[BAR])
            pl2.add("sync", lambda s: s.dma_start(out=b1s[:, :], in_=b1c),
                    inc="wBs", dn=16, waits=[BAR])
            pl2.add("gpsimd", lambda g: g.dma_start(out=b2bf[:, :], in_=b2r),
                    inc="wB", dn=16, waits=[BAR])
            EV_WB = [("wB", pl2.counts["wB"]), ("wBs", pl2.counts["wBs"])]
            ev_wsc = pl2.add("gpsimd", lambda g: g.tensor_tensor(
                wo[:, :], wo[:, :],
                nwb[:, 0:1].broadcast_to([128, H * DM]), ALU.mult),
                inc="g2", waits=[("wB", pl2.counts["wB"])])

            st_free = [None, None]
            w2rdy = [None] * FCH
            for c in range(FCH):
                srcp = w2T[c * 128:(c + 1) * 128, :]
                sl = c % 2
                ev_ld = pl2.add("sync", lambda s, sr=srcp, sl_=sl: s.dma_start(
                    out=stage[sl_][:, :], in_=sr), inc=f"su{sl}", dn=16,
                    waits=[st_free[sl], BAR])
                pl2.add("vector", lambda v, sl_=sl: v.tensor_single_scalar(
                    tmpb[:, :], stage[sl_][:, :], thrb[:, 3:4], ALU.is_lt),
                    inc="v2", waits=[ev_ld, BAR])
                ev_w = pl2.add("vector", lambda v, c_=c, sl_=sl: v.scalar_tensor_tensor(
                    w2t[:, c_ * DM:(c_ + 1) * DM], stage[sl_][:, :],
                    thrb[:, 2:3], tmpb[:, :], ALU.is_gt, ALU.subtract),
                    inc="v2")
                st_free[sl] = ev_w
                w2rdy[c] = ev_w

            xnb_free = [None, None]
            onb_free = [None, None]
            regs2 = {}

            def _mk_load2(eng):
                regs2[0] = nc.values_load(hsl[0:1, 0:1],
                                          engines=[mybir.EngineType.SP],
                                          skip_runtime_bounds_check=True)
                regs2[1] = nc.values_load(hsl[0:1, 1:2],
                                          engines=[mybir.EngineType.SP],
                                          skip_runtime_bounds_check=True)
                return eng.nop()
            pl2.add("sync", _mk_load2, waits=[BAR])

            def onb_fn(dst, s0, s1, sem):
                def f(eng):
                    with eng.If(regs2[0]):
                        eng.wait_ge(sems["cc1"], 1)
                        eng.dma_start(out=dst, in_=s0).then_inc(sems[sem], 16)
                    with eng.Else():
                        eng.wait_ge(sems["cc2"], 1)
                        eng.dma_start(out=dst, in_=s1).then_inc(sems[sem], 16)
                    return eng.nop()
                return f
            xr_free = [None, None]
            out_free = [None, None]
            psH_ev = [None, None]
            hb_free = [None, None, None]
            psS_ev = None
            psO2_ev = [None, None]
            gsb_last = [None, None]
            sgt_ev = None

            for i in range(NTB):
                t0 = i * TB
                par = i % 2
                src = xnT_d[:, t0:t0 + TB].rearrange("(c p) t -> p c t", p=128)
                dap = xnb[par][:, :].rearrange("p (c t) -> p c t", c=KC)
                ev_xnb = pl2.add("sync", lambda s, d=dap, sr=src: s.dma_start(
                    out=d, in_=sr), inc=f"bx{par}", dn=16,
                    waits=[xnb_free[par], None if i > 1 else ("fx0", finalsA["fx0"]),
                           None if i > 1 else ("fx1", finalsA["fx1"])])
                s0 = onT_g[0][:, t0:t0 + TB].rearrange("(h p) t -> p h t", p=128)
                s1 = onT_g[1][:, t0:t0 + TB].rearrange("(h p) t -> p h t", p=128)
                dap = onb[par][:, :].rearrange("p (h t) -> p h t", h=H)
                pl2.counts[f"bo{par}"] += 16
                ev_onb = (f"bo{par}", pl2.counts[f"bo{par}"])
                pl2.add("sync", onb_fn(dap, s0, s1, f"bo{par}"),
                        waits=[onb_free[par]])
                ev_xr = [None, None]
                for j2 in range(2):
                    src = xh_d[t0 + j2 * 128:t0 + j2 * 128 + 128, :]
                    ev_xr[j2] = pl2.add("gpsimd", lambda g, d=xrs[j2], sr=src: g.dma_start(
                        out=d[:, :], in_=sr), inc=f"br{j2}", dn=16,
                        waits=[xr_free[j2], BAR if i == 0 else None])

                ev_wo0 = None
                for he in range(H):
                    for n in range(2):
                        ev_wo0 = pl2.add("tensor", lambda t, he_=he, n_=n, p_=par: t.matmul(
                            psS[:, n_ * 512:(n_ + 1) * 512],
                            onb[p_][:, he_ * TB: he_ * TB + 128],
                            wo[:, he_ * DM + n_ * 512: he_ * DM + n_ * 512 + 512],
                            start=(he_ == 0), stop=(he_ == H - 1)),
                            inc=("t2" if (he == H - 1 and n == 1) else None),
                            waits=([ev_onb, psS_ev, ev_wsc] if he == 0 and n == 0 else []))

                gel_evs = {}

                def emit_ffn2(c2):
                    for j2 in range(2):
                        wts = [gel_evs[c2]]
                        if c2 == 0:
                            wts.append(psO2_ev[j2])
                            psO2_ev[j2] = None
                        if i == 0:
                            wts.append(w2rdy[c2])
                        for n in range(2):
                            pl2.add("tensor", lambda t, c_=c2, j_=j2, n_=n, sl_=c2 % 3: t.matmul(
                                psO2[j_][:, n_ * 512:(n_ + 1) * 512],
                                hbuf[sl_][:, j_ * 128:(j_ + 1) * 128],
                                w2t[:, c_ * DM + n_ * 512: c_ * DM + n_ * 512 + 512],
                                start=(c_ == 0), stop=False),
                                inc=("t2" if (j2 == 1 and n == 1) else None),
                                waits=(wts if n == 0 else []))

                for c in range(FCH):
                    sl = c % 3
                    ev_mm = None
                    for kc in range(KC):
                        ev_mm = pl2.add("tensor", lambda t, c_=c, kc_=kc, sl_=sl, p_=par: t.matmul(
                            psH[c_ % 2][:, :], w1t[:, kc_ * FF + c_ * 128: kc_ * FF + c_ * 128 + 128],
                            xnb[p_][:, kc_ * TB:(kc_ + 1) * TB],
                            start=(kc_ == 0), stop=(kc_ == KC - 1)),
                            inc=("t2" if kc == KC - 1 else None),
                            waits=([ev_xnb, psH_ev[c % 2], EV_WB] if kc == 0 else []))
                    if c == FCH - 1:
                        ev_ffn1_last = ev_mm
                    ev_gel = pl2.add("scalar", lambda s, c_=c, sl_=sl: s.activation(
                        hbuf[sl_][:, :], psH[c_ % 2][:, :], AF.Gelu, bias=b1s[:, c_:c_ + 1]),
                        inc="a2", waits=[ev_mm, hb_free[sl]])
                    gel_evs[c] = ev_gel
                    psH_ev[c % 2] = ev_gel
                    if c >= 2:
                        emit_ffn2(c - 2)
                        hb_free[(c - 2) % 3] = ("t2", pl2.counts["t2"])
                for c2 in (FCH - 2, FCH - 1):
                    emit_ffn2(c2)
                    hb_free[c2 % 3] = ("t2", pl2.counts["t2"])
                ev_ffn2 = None
                for j2 in range(2):
                    for n in range(2):
                        ev_ffn2 = pl2.add("tensor", lambda t, j_=j2, n_=n: t.matmul(
                            psO2[j_][:, n_ * 512:(n_ + 1) * 512], ones_rb[:, :],
                            b2bf[:, n_ * 512:n_ * 512 + 512], start=False, stop=True),
                            inc=("t2" if (j2 == 1 and n == 1) else None))
                ev_wo = [ev_wo0, None]
                for j2 in range(2):
                    ev_si = pl2.add("scalar", lambda s, j_=j2: s.activation(
                        gsb[:, j_ * DM:(j_ + 1) * DM], psO2[j_][:, :], AF.Silu),
                        inc="a2", waits=[ev_ffn2, gsb_last[j2]])
                    psO2_ev[j2] = ev_si
                    ev_sg = pl2.add("vector", lambda v, j_=j2: v.tensor_tensor(
                        sgt[:, :], psS[:, :], gsb[:, j_ * DM:(j_ + 1) * DM],
                        ALU.mult), inc="v2", waits=[ev_wo[j2], ev_si, sgt_ev])
                    gsb_last[j2] = ev_sg
                    if j2 == 0:
                        ev_w1 = None
                        for he in range(H):
                            for n in range(2):
                                ev_w1 = pl2.add("tensor", lambda t, he_=he, n_=n, p_=par: t.matmul(
                                    psS[:, n_ * 512:(n_ + 1) * 512],
                                    onb[p_][:, he_ * TB + 128: he_ * TB + 256],
                                    wo[:, he_ * DM + n_ * 512: he_ * DM + n_ * 512 + 512],
                                    start=(he_ == 0), stop=(he_ == H - 1)),
                                    inc=("t2" if (he == H - 1 and n == 1) else None),
                                    waits=([ev_sg] if he == 0 and n == 0 else []))
                        ev_wo[1] = ev_w1
                    else:
                        psS_ev = ev_sg
                    ev_add = pl2.add("gpsimd", lambda g, j_=j2: g.tensor_add(
                        outs[j_][:, :], sgt[:, :], xrs[j_][:, :]), inc="g2",
                        waits=[ev_sg, ev_xr[j2], out_free[j2]])
                    sgt_ev = ev_add
                    xr_free[j2] = ev_add
                    dstap = out_d[t0 + j2 * 128:t0 + j2 * 128 + 128, :]
                    ev_st = pl2.add("scalar", lambda s, d=dstap, sr=outs[j2]: s.dma_start(
                        out=d, in_=sr[:, :]), inc=f"bs{j2}", dn=16, waits=[ev_add])
                    out_free[j2] = ev_st
                xnb_free[par] = ev_ffn1_last
                onb_free[par] = ev_wo

            finalsB = dict(pl2.counts)

            with nc.Block() as blkB:
                @blkB.sync
                def _(engx):
                    pl2.realize("sync", engx)

                @blkB.gpsimd
                def _(engx):
                    pl2.realize("gpsimd", engx)
                    engx.wait_ge(sems["bs0"], finalsB["bs0"])
                    engx.wait_ge(sems["bs1"], finalsB["bs1"])

                @blkB.vector
                def _(engx):
                    pl2.realize("vector", engx)

                @blkB.scalar
                def _(engx):
                    pl2.realize("scalar", engx)

                @blkB.tensor
                def _(engx):
                    pl2.realize("tensor", engx)

    return nc


_NC_CACHE = None


def _get_nc():
    global _NC_CACHE
    if _NC_CACHE is None:
        _NC_CACHE = build()
    return _NC_CACHE


def kernel(L0_attn_hidden_states, L0_attn_Wq, L0_attn_Wk, L0_attn_Wv, L0_attn_Wo,
           L0_attn_norm_w, L0_ffn_W1, L0_ffn_b1, L0_ffn_W2, L0_ffn_b2):
    x = np.ascontiguousarray(np.asarray(L0_attn_hidden_states, dtype=np.float32))
    wq = np.ascontiguousarray(np.asarray(L0_attn_Wq, dtype=np.float32).T)
    wk = np.ascontiguousarray(np.asarray(L0_attn_Wk, dtype=np.float32).T)
    wv = np.ascontiguousarray(np.asarray(L0_attn_Wv, dtype=np.float32).T)
    wo = np.ascontiguousarray(np.asarray(L0_attn_Wo, dtype=np.float32).T)
    w1 = np.ascontiguousarray(np.asarray(L0_ffn_W1, dtype=np.float32).T)
    w2 = np.ascontiguousarray(np.asarray(L0_ffn_W2, dtype=np.float32).T)
    b1 = np.ascontiguousarray(np.asarray(L0_ffn_b1, dtype=np.float32).reshape(FCH, 128).T)
    b2 = np.ascontiguousarray(np.asarray(L0_ffn_b2, dtype=np.float32).reshape(1, DM))
    nw = np.ascontiguousarray(
        np.asarray(L0_attn_norm_w, dtype=np.float32).reshape(128, 1))

    nc = _get_nc()
    in_maps = []
    for c in range(8):
        b = c // 2
        half = c % 2
        es = slice(half * (HL * DH), (half + 1) * (HL * DH))
        in_maps.append({
            "x": np.ascontiguousarray(x[b]),
            "wqT": np.ascontiguousarray(wq[:, es]),
            "wkT": np.ascontiguousarray(wk[:, es]),
            "wvT": np.ascontiguousarray(wv[:, es]),
            "woT": wo, "w1T": w1, "w2T": w2, "b1c": b1, "b2r": b2, "nwb": nw,
            "xh": np.ascontiguousarray(x[b, half * (S // 2):(half + 1) * (S // 2)]),
            "hsl": np.array([[1 - half, half]], dtype=np.int32),
        })
    res = run_bass_kernel_spmd(nc, in_maps, core_ids=list(range(8)))
    out = np.empty((B, S, DM), dtype=np.float32)
    for b in range(B):
        out[b, 0:S // 2] = res.results[2 * b]["out"]
        out[b, S // 2:] = res.results[2 * b + 1]["out"]
    return out
